# version 20
# baseline (speedup 1.0000x reference)
import math
import numpy as np
import ml_dtypes
from contextlib import ExitStack

import concourse.bass as bass
import concourse.mybir as mybir
from concourse.bass_utils import run_bass_kernel_spmd

F32 = mybir.dt.float32
BF16 = mybir.dt.bfloat16
I32 = mybir.dt.int32
AF = mybir.ActivationFunctionType
ALU = mybir.AluOpType
AX = mybir.AxisListType

D = 1024
HEADS = 16
KVH = 4
HD = 64
KVD = 256
WIN = 64
IDIM = 2816
NCHUNK = IDIM // 128
FCH = D // 128
NCORES = 8
EPS = 1e-5
SCALE = HD ** -0.5
QMAX = 127.0
NEG = -1.0e30


class Sem:

    def __init__(self, ctx, nc, name):
        self.h = ctx.enter_context(nc.semaphore(name))
        self.n = 0

    def inc(self, instr, k=1):
        instr.then_inc(self.h, k)
        self.n += k
        return instr


def build_program(T=2048, n_spill=6):
    assert T % 512 == 0
    NT = (T + 64) // 128
    NTT = NT + 1
    QQ = T // 128
    T4 = T // 512
    KT = T + 64
    XP = T + 128
    NG = (2 * T) // WIN
    n_keep = NCHUNK - n_spill

    nc = bass.Bass("TRN2", target_bir_lowering=False, debug=False)

    x_ext = nc.dram_tensor("x_ext", [KT, D], F32, kind="ExternalInput").ap()
    x_glob = nc.dram_tensor("x_glob", [NG, D], F32, kind="ExternalInput").ap()
    Wq = nc.dram_tensor("Wq", [D, D], F32, kind="ExternalInput").ap()
    Wk = nc.dram_tensor("Wk", [D, KVD], F32, kind="ExternalInput").ap()
    Wv = nc.dram_tensor("Wv", [D, KVD], F32, kind="ExternalInput").ap()
    Wo = nc.dram_tensor("Wo", [D, D], F32, kind="ExternalInput").ap()
    wg = nc.dram_tensor("wg", [IDIM, D], F32, kind="ExternalInput").ap()
    wu = nc.dram_tensor("wu", [IDIM, D], F32, kind="ExternalInput").ap()
    wd = nc.dram_tensor("wd", [D, IDIM], F32, kind="ExternalInput").ap()
    wdT = nc.dram_tensor("wdT", [IDIM, D], F32, kind="ExternalInput").ap()
    clip_x = nc.dram_tensor("clip_x", [1, 1], F32, kind="ExternalInput").ap()
    clip_d = nc.dram_tensor("clip_d", [1, 1], F32, kind="ExternalInput").ap()
    masks = nc.dram_tensor("masks", [HEADS, 128, 192], F32,
                           kind="ExternalInput").ap()
    masks_q0 = nc.dram_tensor("masks_q0", [HEADS, 128, 192], F32,
                              kind="ExternalInput").ap()
    y = nc.dram_tensor("y", [T, D], F32, kind="ExternalOutput").ap()

    wgq_s = nc.dram_tensor("wgq_s", [IDIM, D], BF16, kind="Internal").ap()
    wuq_s = nc.dram_tensor("wuq_s", [IDIM, D], BF16, kind="Internal").ap()
    wdqT_s = nc.dram_tensor("wdqT_s", [IDIM, D], BF16, kind="Internal").ap()
    xq_s = nc.dram_tensor("xq_s", [XP, D], BF16, kind="Internal").ap()
    h_sp = nc.dram_tensor("h_sp", [max(1, n_spill) * 128, T], F32,
                          kind="Internal").ap()
    swd_s = nc.dram_tensor("swd_s", [1, D], F32, kind="Internal").ap()
    swdr_s = nc.dram_tensor("swdr_s", [1, D], F32, kind="Internal").ap()
    sc1_s = nc.dram_tensor("sc1_s", [1, 8], F32, kind="Internal").ap()
    sc2_s = nc.dram_tensor("sc2_s", [1, 8], F32, kind="Internal").ap()
    cc1_in = nc.dram_tensor("cc1_in", [1, 8], F32, kind="Internal").ap()
    cc1_out = nc.dram_tensor("cc1_out", [1, 8], F32, kind="Internal").ap()
    cc2_in = nc.dram_tensor("cc2_in", [1, 8], F32, kind="Internal").ap()
    cc2_out = nc.dram_tensor("cc2_out", [1, 8], F32, kind="Internal").ap()

    ident_bf_d = nc.inline_tensor(np.eye(128, dtype=ml_dtypes.bfloat16),
                                  name="ident_bf_c").ap()
    ident_f_d = nc.inline_tensor(np.eye(128, dtype=np.float32),
                                 name="ident_f_c").ap()

    S, A, V, P, G = nc.sync, nc.scalar, nc.vector, nc.tensor, nc.gpsimd

    ctx = ExitStack()
    with ctx:
        def sb(name, shape, dt, side=None):
            return ctx.enter_context(nc.sbuf_tensor(name, shape, dt,
                                                    side=side))

        def psb(name):
            return ctx.enter_context(nc.psum_tensor(name, [128, 512], F32))

        ident_bf = sb("ident_bf", [128, 128], BF16)
        ident_f = sb("ident_f", [128, 128], F32)
        c0125 = sb("c0125", [128, 1], F32)
        epsc = sb("epsc", [128, 1], F32)
        clipx_sb = sb("clipx_sb", [1, 1], F32)
        clipd_sb = sb("clipd_sb", [1, 1], F32)
        rstd_sl = sb("rstd_sl", [128, NTT + 1], F32)
        nmrs_sl = sb("nmrs_sl", [128, NTT + 1], F32)
        rstd2 = sb("rstd2", [128, 1], F32)
        nmrs2 = sb("nmrs2", [128, 1], F32)
        stats6 = sb("stats6", [128, 12], F32)
        mvc = sb("mvc", [128, 2], F32)
        sd_c = sb("sd_c", [128, 1], F32)
        gacc = sb("gacc", [128, 1], F32)
        hacc = sb("hacc", [128, 1], F32)
        amx_c = sb("amx_c", [128, 1], F32)
        red1 = sb("red1", [1, 8], F32)
        red2 = sb("red2", [1, 8], F32)
        gmax_sb = sb("gmax_sb", [1, 8], F32)
        hmax_sb = sb("hmax_sb", [1, 8], F32)
        sig_c = sb("sig_c", [1, 1], F32)
        sx_c = sb("sx_c", [1, 1], F32)
        rsx_c = sb("rsx_c", [1, 1], F32)
        sh_c = sb("sh_c", [1, 1], F32)
        rsh_c = sb("rsh_c", [1, 1], F32)
        bc1 = sb("bc1", [128, 8], F32)
        bc2 = sb("bc2", [128, 8], F32)
        swg_all = sb("swg_all", [128, NCHUNK], F32)
        swu_all = sb("swu_all", [128, NCHUNK], F32)
        sug_all = sb("sug_all", [128, NCHUNK], F32)
        suu_all = sb("suu_all", [128, NCHUNK], F32)
        wamax = sb("wamax", [128, 1], F32)
        wam2 = sb("wam2", [128, 1], F32)
        wrec = sb("wrec", [128, 1], F32)
        wsc127 = sb("wsc127", [128, 1], F32)

        pb = [psb(f"pb{i}") for i in range(8)]
        pb_bf = [b.ap().bitcast(BF16) for b in pb]

        names = ("ld0 ld1 ldc ldwa ldff0 ldff1 ldff2 sVa sAa sVb sAb sVc sPt sVd "
                 "sPq sVq gdup cc1s cc2s cc1c cc2c sPs sVs sAe sVp sPtr sAt sPav sVav "
                 "sVw sAw stw sAq sVq2 stq trx sPo sVo sty trw sPf sAf sVf "
                 "gsp trd grl sAj sVj sPd sVe acc gsc ssc lxa lxb ldi sVx sAx trw1 trw2 stwb sVh sPu sAu").split()
        sm = {n: Sem(ctx, nc, n) for n in names}
        (ld0, ld1, ldc, ldwa, ldff0, ldff1, ldff2, sVa, sAa, sVb, sAb, sVc, sPt,
         sVd, sPq, sVq, gdup, cc1s, cc2s, cc1c, cc2c, sPs, sVs, sAe, sVp, sPtr, sAt,
         sPav, sVav, sVw, sAw, stw, sAq, sVq2, stq, trx, sPo, sVo, sty,
         trw, sPf, sAf, sVf, gsp, trd, grl, sAj, sVj, sPd, sVe, acc, gsc,
         ssc, lxa, lxb, ldi, sVx, sAx, trw1, trw2, stwb, sVh, sPu, sAu) = (sm[n] for n in names)
        trwl = [trw, trw1, trw2]

        def VC(instr):
            sVx.inc(instr)
            V.wait_ge(sVx.h, sVx.n)
            return instr

        def AC(instr):
            sAx.inc(instr)
            A.wait_ge(sAx.h, sAx.n)
            return instr

        def scope_barrier(*sems):
            for _eng in (V, A, S, P, G):
                for _s in sems:
                    _eng.wait_ge(_s.h, _s.n)

        ldi.inc(S.dma_start(out=ident_bf[:], in_=ident_bf_d), 16)
        ldi.inc(S.dma_start(out=ident_f[:], in_=ident_f_d), 16)
        ldc.inc(S.dma_start(out=clipx_sb[:], in_=clip_x), 16)
        ldc.inc(S.dma_start(out=clipd_sb[:], in_=clip_d), 16)
        V.memset(c0125[:], SCALE)
        V.memset(epsc[:], EPS)
        V.memset(gacc[:], 0.0)
        V.memset(hacc[:], 0.0)
        V.memset(red1[:], 0.0)
        V.memset(red2[:], 0.0)

        sattn = ExitStack()
        attn_fm = sattn.enter_context(
            nc.sbuf_tensor("attn_fm", [128, FCH * T], BF16))

        s2 = ExitStack()
        sb2 = lambda name, shape, dt: s2.enter_context(
            nc.sbuf_tensor(name, shape, dt))
        sb2r = lambda name, shape, dt: s2.enter_context(
            nc.sbuf_tensor(name, shape, dt, side="right"))
        q_fm = sb2("q_fm", [128, FCH * T], BF16)
        k_dup = sb2("k_dup", [128, KVH * KT], BF16)
        kg_dup = sb2r("kg_dup", [128, KVH * NG], BF16)
        v_tm = sb2("v_tm", [128, NTT * KVD], BF16)
        vg_tm = sb2r("vg_tm", [max(NG, 1), KVD], BF16)
        masks_sb = sb2r("masks_sb", [128, HEADS * 192], BF16)
        masksq0_sb = sb2r("masksq0_sb", [128, HEADS * 192], BF16)
        p_all = [sb2r(f"p_all{i}", [128, 512], BF16) for i in range(2)]
        pn_all = [sb2r(f"pn_all{i}", [128, 512], BF16) for i in range(2)]
        pt1s = [sb2r(f"pt1s{i}", [128, 256], BF16) for i in range(2)]
        pt2s = [sb2r(f"pt2s{i}", [64, 256], BF16) for i in range(2)]
        ptgs = [sb2r(f"ptgs{i}", [64, 256], BF16) for i in range(2)]
        mx_c = [sb2(f"mx_c{i}", [128, 2], F32) for i in range(2)]
        nmx_c = [sb2(f"nmx_c{i}", [128, 2], F32) for i in range(2)]
        sums_c = [sb2(f"sums_c{i}", [128, 4], F32) for i in range(2)]
        rec_c = [sb2(f"rec_c{i}", [128, 4], F32) for i in range(2)]
        wbuf = [sb2r(f"wbuf{i}", [128, 1024], F32) for i in range(3)]
        wi_s = [sb2r(f"wi_s{i}", [128, 256], I32) for i in range(2)]
        wqb_s = [sb2r(f"wqb_s{i}", [128, 256], BF16) for i in range(2)]
        swdr_row = sb2r("swdr_row", [128, D], F32)

        s1 = ExitStack()
        sb1 = lambda name, shape, dt: s1.enter_context(
            nc.sbuf_tensor(name, shape, dt))
        xtile = [sb1(f"xtile{i}", [128, D], F32) for i in range(2)]
        xnf = sb1("xnf", [128, D], F32)
        xnb = [sb1(f"xnb{i}", [128, D], BF16) for i in range(2)]
        xn_fm = sb1("xn_fm", [128, FCH * KT], BF16)
        xng_fm = sb1("xng_fm", [128, FCH * NG], BF16)
        wq_sb = sb1("wq_sb", [128, FCH * D], BF16)
        wk_sb = sb1("wk_sb", [128, FCH * KVD], BF16)
        wv_sb = sb1("wv_sb", [128, FCH * KVD], BF16)

        for f in range(FCH):
            ldwa.inc(G.dma_start(out=wq_sb[:, f * D:(f + 1) * D],
                                 in_=Wq[f * 128:(f + 1) * 128, :]), 16)
            ldwa.inc(G.dma_start(out=wk_sb[:, f * KVD:(f + 1) * KVD],
                                 in_=Wk[f * 128:(f + 1) * 128, :]), 16)
            ldwa.inc(G.dma_start(out=wv_sb[:, f * KVD:(f + 1) * KVD],
                                 in_=Wv[f * 128:(f + 1) * 128, :]), 16)
        for hh in range(HEADS):
            ldc.inc(G.dma_start(
                out=masks_sb[:, hh * 192:(hh + 1) * 192],
                in_=masks[hh]), 16)
            ldc.inc(G.dma_start(
                out=masksq0_sb[:, hh * 192:(hh + 1) * 192],
                in_=masks_q0[hh]), 16)

        _sc_PHA = nc.enter_named_scope("PHA", False)
        def tile_rows(j):
            return 128 if j < NT else 64

        ld_par = [ld0, ld1]

        def pha_load(j, gate=None):
            if j > NTT:
                return
            r2 = tile_rows(j) if j <= NT else NG
            buf2 = j % 2
            if gate is not None:
                S.wait_ge(sAb.h, gate)
            src2 = x_ext[j * 128: j * 128 + r2, :] if j <= NT else x_glob
            for q4 in range(0, D, 256):
                ld_par[buf2].inc(S.dma_start(
                    out=xtile[buf2][0:r2, q4:q4 + 256],
                    in_=src2[:, q4:q4 + 256]), 16)

        pha_load(0)
        pha_load(1)
        n_amax = 0
        for j in range(NTT + 1):
            r = tile_rows(j) if j <= NT else NG
            buf = j % 2
            V.wait_ge(ld_par[buf].h, ld_par[buf].n)
            V.bn_stats(stats6[0:r, 0:6], xtile[buf][0:r, 0:512])
            VC(V.bn_stats(stats6[0:r, 6:12], xtile[buf][0:r, 512:1024]))
            sVa.inc(V.bn_aggr(mvc[0:r, :], stats6[0:r, :]))
            A.wait_ge(sVa.h, sVa.n)
            sAa.inc(A.activation(out=sd_c[0:r, :], in_=mvc[0:r, 1:2],
                                 func=AF.Sqrt, bias=epsc[0:r, :], scale=1.0))
            V.wait_ge(sAa.h, sAa.n)
            VC(V.reciprocal(rstd_sl[0:r, j:j + 1], sd_c[0:r, :]))
            VC(V.tensor_tensor(out=nmrs_sl[0:r, j:j + 1], in0=mvc[0:r, 0:1],
                               in1=rstd_sl[0:r, j:j + 1], op=ALU.mult))
            sVb.inc(V.tensor_scalar(out=nmrs_sl[0:r, j:j + 1],
                                    in0=nmrs_sl[0:r, j:j + 1], scalar1=-1.0,
                                    scalar2=None, op0=ALU.mult))
            A.wait_ge(sVb.h, sVb.n)
            A.wait_ge(sVc.h, n_amax)
            if j >= 2:
                A.wait_ge(sPt.h, 8 * (j - 1))
            AC(A.activation(out=xnf[0:r, :], in_=xtile[buf][0:r, :],
                            func=AF.Identity, bias=nmrs_sl[0:r, j:j + 1],
                            scale=rstd_sl[0:r, j:j + 1]))
            sAb.inc(A.copy(xnb[buf][0:r, :], xnf[0:r, :]))
            pha_load(j + 2, gate=sAb.n)
            V.wait_ge(sAb.h, sAb.n)
            if j <= NT:
                own = (64, 128) if j == 0 else ((0, 64) if j == NT else (0, 128))
                VC(V.tensor_reduce(out=amx_c[own[0]:own[1], :],
                                   in_=xnf[own[0]:own[1], :], axis=AX.X,
                                   op=ALU.max, apply_absolute_value=True))
                V.wait_ge(sVc.h, sVc.n)
                sVc.inc(V.tensor_tensor(out=gacc[own[0]:own[1], :],
                                        in0=gacc[own[0]:own[1], :],
                                        in1=amx_c[own[0]:own[1], :],
                                        op=ALU.max))
                n_amax += 1
            P.wait_ge(sAb.h, sAb.n)
            if j == 0:
                P.wait_ge(ldi.h, 32)
            for f in range(FCH):
                P.wait_ge(sVd.h, max(0, sVd.n - 1))
                sPt.inc(P.transpose(pb_bf[f % 2][0:128, 0:r],
                                    xnb[buf][0:r, f * 128:(f + 1) * 128],
                                    ident_bf[0:r, 0:r]))
                V.wait_ge(sPt.h, sPt.n)
                if j <= NT:
                    dst = xn_fm[:, f * KT + j * 128: f * KT + j * 128 + r]
                else:
                    dst = xng_fm[:, f * NG:(f + 1) * NG]
                sVd.inc(V.tensor_copy(dst, pb_bf[f % 2][0:128, 0:r]))

        nc.leave_named_scope("PHA", _sc_PHA[0], False)
        _sc_PHB = nc.enter_named_scope("PHB", False)
        P.wait_ge(sVc.h, sVc.n)
        sPt.inc(P.transpose(pb[2].ap()[0:1, 0:128], gacc[:], ident_f[:]))
        V.wait_ge(sPt.h, sPt.n)
        sVc.inc(V.tensor_reduce(out=red1[0:1, 0:1],
                                in_=pb[2].ap()[0:1, 0:128], axis=AX.X,
                                op=ALU.max))
        G.wait_ge(sVc.h, sVc.n)
        cc1s.inc(G.dma_start(out=cc1_in, in_=red1[:]), 16)
        G.wait_ge(cc1s.h, cc1s.n)
        cc1c.inc(G.collective_compute(
            "AllReduce", ALU.max, replica_groups=[list(range(NCORES))],
            ins=[cc1_in], outs=[cc1_out]))
        G.wait_ge(cc1c.h, cc1c.n)
        cc1s.inc(G.dma_start(out=gmax_sb[:], in_=cc1_out), 16)

        nc.leave_named_scope("PHB", _sc_PHB[0], False)
        _sc_PHC = nc.enter_named_scope("PHC", False)
        P.wait_ge(sVd.h, sVd.n)
        P.wait_ge(ldwa.h, ldwa.n)

        last_evac = {0: 0, 1: 0}

        def proj_group(out_dst, lhs, rhs, bank_idx, rows=128):
            bank = pb[bank_idx]
            width = rhs[0].shape[-1]
            P.wait_ge(sVq.h, last_evac[bank_idx])
            mm = None
            for i in range(len(lhs)):
                mm = P.matmul(bank.ap()[0:rows, 0:width], lhs[i], rhs[i],
                              start=(i == 0), stop=(i == len(lhs) - 1))
            sPq.inc(mm)
            V.wait_ge(sPq.h, sPq.n)
            sVq.inc(V.tensor_copy(out_dst, bank.ap()[0:rows, 0:width]))
            last_evac[bank_idx] = sVq.n

        bk = 0
        for m in range(FCH):
            for t in range(T4):
                lhs = [wq_sb[:, f * D + m * 128:(f * D + (m + 1) * 128)]
                       for f in range(FCH)]
                rhs = [xn_fm[:, f * KT + 64 + t * 512:
                             f * KT + 64 + (t + 1) * 512] for f in range(FCH)]
                proj_group(q_fm[:, m * T + t * 512: m * T + (t + 1) * 512],
                           lhs, rhs, bk % 2)
                bk += 1
        def proj_group2(dst_a, dst_b, lhs, rhs, bank_idx):
            bank = pb[bank_idx]
            width = rhs[0].shape[-1]
            P.wait_ge(sVq.h, last_evac[bank_idx])
            mm = None
            for i in range(len(lhs)):
                mm = P.matmul(bank.ap()[:, 0:width], lhs[i], rhs[i],
                              start=(i == 0), stop=(i == len(lhs) - 1))
            sPq.inc(mm)
            V.wait_ge(sPq.h, sPq.n)
            V.tensor_copy(dst_a, bank.ap()[0:64, 0:width])
            sVq.inc(V.tensor_copy(dst_b, bank.ap()[64:128, 0:width]))
            last_evac[bank_idx] = sVq.n

        kslices = [(t * 512, 512) for t in range(KT // 512)] + [(KT - 64, 64)]
        for m in range(2):
            for (o, w) in kslices:
                lhs = [wk_sb[:, f * KVD + m * 128: f * KVD + (m + 1) * 128]
                       for f in range(FCH)]
                rhs = [xn_fm[:, f * KT + o: f * KT + o + w] for f in range(FCH)]
                proj_group2(k_dup[0:64, 2 * m * KT + o: 2 * m * KT + o + w],
                            k_dup[64:128, (2 * m + 1) * KT + o:
                                  (2 * m + 1) * KT + o + w],
                            lhs, rhs, bk % 2)
                bk += 1
        for m in range(2):
            lhs = [wk_sb[:, f * KVD + m * 128: f * KVD + (m + 1) * 128]
                   for f in range(FCH)]
            rhs = [xng_fm[:, f * NG:(f + 1) * NG] for f in range(FCH)]
            proj_group2(kg_dup[0:64, 2 * m * NG:(2 * m + 1) * NG],
                        kg_dup[64:128, (2 * m + 1) * NG:(2 * m + 2) * NG],
                        lhs, rhs, bk % 2)
            bk += 1
        for j in range(NTT):
            r = tile_rows(j)
            lhs = [xn_fm[:, f * KT + j * 128: f * KT + j * 128 + r]
                   for f in range(FCH)]
            rhs = [wv_sb[:, f * KVD:(f + 1) * KVD] for f in range(FCH)]
            proj_group(v_tm[0:r, j * KVD:(j + 1) * KVD], lhs, rhs, bk % 2,
                       rows=r)
            bk += 1
        lhs = [xng_fm[:, f * NG:(f + 1) * NG] for f in range(FCH)]
        rhs = [wv_sb[:, f * KVD:(f + 1) * KVD] for f in range(FCH)]
        proj_group(vg_tm[:], lhs, rhs, bk % 2, rows=NG)

        G.wait_ge(sVq.h, sVq.n)
        for g in range(KVH):
            if g % 2 == 0:
                gdup.inc(G.dma_start(out=k_dup[64:128, g * KT:(g + 1) * KT],
                                     in_=k_dup[0:64, g * KT:(g + 1) * KT]),
                         16)
                gdup.inc(G.dma_start(out=kg_dup[64:128, g * NG:(g + 1) * NG],
                                     in_=kg_dup[0:64, g * NG:(g + 1) * NG]),
                         16)
            else:
                gdup.inc(G.dma_start(out=k_dup[0:64, g * KT:(g + 1) * KT],
                                     in_=k_dup[64:128, g * KT:(g + 1) * KT]),
                         16)
                gdup.inc(G.dma_start(out=kg_dup[0:64, g * NG:(g + 1) * NG],
                                     in_=kg_dup[64:128, g * NG:(g + 1) * NG]),
                         16)

        nc.leave_named_scope("PHC", _sc_PHC[0], False)
        s1.close()

        jobs = []
        s_pieces = ((0, 1024), (1024, 1024), (2048, IDIM - 2048))
        for c in range(FCH):
            for pc, (o, w) in enumerate(s_pieces):
                jobs.append(("sp", wd, c, pc, o, w))
        n_sp = len(jobs)
        for c in range(NCHUNK):
            jobs.append(("q", wg, wgq_s, c, swg_all, D))
            jobs.append(("q", wu, wuq_s, c, swu_all, D))

        st_state = {"load_idx": 0, "quant_idx": 0,
                    "a_done": [], "v_done": []}

        NR = 3
        ldffs = [ldff0, ldff1, ldff2]
        stwl = [stw, stwb]

        def emit_job_load(i):
            if i >= len(jobs):
                return
            job = jobs[i]
            if i >= NR:
                S.wait_ge(sAw.h, st_state["a_done"][i - NR])
                S.wait_ge(sVw.h, st_state["v_done"][i - NR])
            par = ldffs[i % NR]
            if job[0] == "q":
                _, src, _, c, _, width = job
                for q4 in range(0, width, 256):
                    par.inc(S.dma_start(
                        out=wbuf[i % NR][:, q4:q4 + 256],
                        in_=src[c * 128:(c + 1) * 128, q4:q4 + 256]), 16)
            elif job[0] == "sp":
                _, src, c, pc, o, w = job
                for q4 in range(0, w, 256):
                    wq4 = min(256, w - q4)
                    par.inc(S.dma_start(
                        out=wbuf[i % NR][:, q4:q4 + wq4],
                        in_=src[c * 128:(c + 1) * 128,
                                o + q4:o + q4 + wq4]), 16)
            else:
                c = job[1]
                for q4 in range(0, D, 256):
                    par.inc(S.dma_start(
                        out=wbuf[i % NR][:, q4:q4 + 256],
                        in_=wdT[c * 128:(c + 1) * 128, q4:q4 + 256]), 16)

        def emit_job_quant(i):
            job = jobs[i]
            buf = i % NR
            par = ldffs[buf]
            V.wait_ge(par.h, par.n)
            if job[0] == "q":
                _, src, dst, c, sc_dst, width = job
                V.wait_ge(gsc.h, gsc.n)
                VC(V.tensor_reduce(out=wamax[:], in_=wbuf[buf][:, 0:width],
                                   axis=AX.X, op=ALU.max,
                                   apply_absolute_value=True))
                VC(V.reciprocal(wrec[:], wamax[:]))
                VC(V.tensor_scalar(out=wsc127[:], in0=wrec[:], scalar1=QMAX,
                                   scalar2=None, op0=ALU.mult))
                sVw.inc(V.tensor_scalar(out=sc_dst[:, c:c + 1], in0=wamax[:],
                                        scalar1=1.0 / QMAX, scalar2=None,
                                        op0=ALU.mult))
                for j, o in enumerate(range(0, width, 256)):
                    V.wait_ge(sAw.h,
                              max(0, sAw.n - 1))
                    sVw.inc(V.scalar_tensor_tensor(
                        out=wi_s[j % 2][:], in0=wbuf[buf][:, o:o + 256],
                        scalar=wsc127[:, 0:1], in1=wbuf[buf][:, o:o + 256],
                        op0=ALU.mult, op1=ALU.bypass))
                    A.wait_ge(sVw.h, sVw.n)
                    stp = stwl[j % 2]
                    A.wait_ge(stp.h, stp.n)
                    sAw.inc(A.copy(wqb_s[j % 2][:], wi_s[j % 2][:]))
                    S.wait_ge(sAw.h, sAw.n)
                    stp.inc(S.dma_start(
                        out=dst[c * 128:(c + 1) * 128, o:o + 256],
                        in_=wqb_s[j % 2][:]), 16)
            elif job[0] == "sp":
                _, src, c, pc, o, w = job
                V.wait_ge(sVw.h, sVw.n)
                if pc == 0:
                    sVw.inc(V.tensor_reduce(out=wamax[:],
                                            in_=wbuf[buf][:, 0:w],
                                            axis=AX.X, op=ALU.max,
                                            apply_absolute_value=True))
                else:
                    VC(V.tensor_reduce(out=wam2[:], in_=wbuf[buf][:, 0:w],
                                       axis=AX.X, op=ALU.max,
                                       apply_absolute_value=True))
                    sVw.inc(V.tensor_tensor(out=wamax[:], in0=wamax[:],
                                            in1=wam2[:], op=ALU.max))
                if pc == 2:
                    V.wait_ge(gsc.h, gsc.n)
                    V.wait_ge(sVw.h, sVw.n)
                    VC(V.tensor_scalar(out=wsc127[:], in0=wamax[:],
                                       scalar1=1.0 / QMAX, scalar2=None,
                                       op0=ALU.mult))
                    VC(V.reciprocal(wrec[:], wamax[:]))
                    sVw.inc(V.tensor_scalar(out=wrec[:], in0=wrec[:],
                                            scalar1=QMAX, scalar2=None,
                                            op0=ALU.mult))
                    G.wait_ge(sVw.h, sVw.n)
                    gsc.inc(G.dma_start(out=swd_s[0:1, c * 128:(c + 1) * 128],
                                        in_=wsc127[:]), 16)
                    gsc.inc(G.dma_start(out=swdr_s[0:1,
                                               c * 128:(c + 1) * 128],
                                        in_=wrec[:]), 16)
            else:
                c = job[1]
                for j, o in enumerate(range(0, D, 256)):
                    V.wait_ge(sAw.h, max(0, sAw.n - 1))
                    sVw.inc(V.tensor_tensor(out=wi_s[j % 2][:],
                                            in0=wbuf[buf][:, o:o + 256],
                                            in1=swdr_row[:, o:o + 256],
                                            op=ALU.mult))
                    A.wait_ge(sVw.h, sVw.n)
                    stp = stwl[j % 2]
                    A.wait_ge(stp.h, stp.n)
                    sAw.inc(A.copy(wqb_s[j % 2][:], wi_s[j % 2][:]))
                    S.wait_ge(sAw.h, sAw.n)
                    stp.inc(S.dma_start(
                        out=wdqT_s[c * 128:(c + 1) * 128, o:o + 256],
                        in_=wqb_s[j % 2][:]), 16)
            st_state["a_done"].append(sAw.n)
            st_state["v_done"].append(sVw.n)

        def pump_wq():
            qi = st_state["quant_idx"]
            if qi >= len(jobs):
                return
            while st_state["load_idx"] <= min(qi + NR - 1, len(jobs) - 1):
                emit_job_load(st_state["load_idx"])
                st_state["load_idx"] += 1
            emit_job_quant(qi)
            st_state["quant_idx"] = qi + 1

        _sc_PHD = nc.enter_named_scope("PHD", False)
        P.wait_ge(gdup.h, gdup.n)
        P.wait_ge(ldc.h, ldc.n)

        units = [(p, qq) for p in range(FCH) for qq in range(QQ)]
        NU = len(units)
        GOF = 192 + NG
        pump_state = {"bcast": False}

        def pump2():
            if (st_state["quant_idx"] >= n_sp and not pump_state["bcast"]):
                pump_state["bcast"] = True
                S.wait_ge(gsc.h, gsc.n)
                ssc.inc(S.dma_start(out=swdr_row[:],
                                    in_=swdr_s.partition_broadcast(128)), 16)
                V.wait_ge(ssc.h, ssc.n)
                for c in range(NCHUNK):
                    jobs.append(("t", c))
            pump_wq()

        def emit_scores(u):
            p, qq = units[u]
            pi = u % 2
            g = p // 2
            scA = pb[pi]
            scB = pb[2 + pi]
            P.wait_ge(sAe.h, max(0, u - 1))
            P.wait_ge(sAt.h, max(0, u - 1))
            lhsA = q_fm[0:64, p * T + qq * 128: p * T + (qq + 1) * 128]
            lhsB = q_fm[64:128, p * T + qq * 128: p * T + (qq + 1) * 128]
            kw = k_dup[:, g * KT + qq * 128: g * KT + qq * 128 + 192]
            msrc = masksq0_sb if qq == 0 else masks_sb
            P.matmul(scA.ap()[:, 0:192], lhsA, kw[0:64, :],
                     start=True, stop=False, skip_group_check=True)
            P.matmul(scA.ap()[:, 0:192], ident_bf[:],
                     msrc[:, (2 * p) * 192:(2 * p + 1) * 192],
                     start=False, stop=True, skip_group_check=True)
            P.matmul(scA.ap()[:, 192:192 + NG], lhsA,
                     kg_dup[0:64, g * NG:(g + 1) * NG],
                     start=False, stop=True, skip_group_check=True)
            P.matmul(scB.ap()[:, 0:192], lhsB, kw[64:128, :],
                     start=True, stop=False, skip_group_check=True)
            P.matmul(scB.ap()[:, 0:192], ident_bf[:],
                     msrc[:, (2 * p + 1) * 192:(2 * p + 2) * 192],
                     start=False, stop=True, skip_group_check=True)
            sPs.inc(P.matmul(scB.ap()[:, 192:192 + NG], lhsB,
                             kg_dup[64:128, g * NG:(g + 1) * NG],
                             start=False, stop=True, skip_group_check=True))

        def emit_exps(u):
            pi = u % 2
            scA = pb[pi]
            scB = pb[2 + pi]
            A.wait_ge(sPs.h, u + 1)
            A.wait_ge(sVp.h, max(0, u - 1))
            A.activation(out=p_all[pi][:, 0:192], in_=scA.ap()[:, 0:192],
                         func=AF.Exp, bias=0.0, scale=c0125[:, 0:1],
                         accum_out=sums_c[pi][:, 0:1])
            A.activation(out=p_all[pi][:, 192:192 + NG],
                         in_=scA.ap()[:, 192:192 + NG],
                         func=AF.Exp, bias=0.0,
                         scale=c0125[:, 0:1], accum_out=sums_c[pi][:, 1:2])
            A.activation(out=p_all[pi][:, GOF:GOF + 192],
                         in_=scB.ap()[:, 0:192],
                         func=AF.Exp, bias=0.0, scale=c0125[:, 0:1],
                         accum_out=sums_c[pi][:, 2:3])
            sAe.inc(A.activation(out=p_all[pi][:, GOF + 192:2 * GOF],
                                 in_=scB.ap()[:, 192:192 + NG],
                                 func=AF.Exp, bias=0.0,
                                 scale=c0125[:, 0:1],
                                 accum_out=sums_c[pi][:, 3:4]))

        def emit_stage_b(u):
            pi = u % 2
            V.wait_ge(sAe.h, u + 1)
            V.wait_ge(sPtr.h, max(0, u - 1))
            VC(V.reciprocal(rec_c[pi][:], sums_c[pi][:]))
            V.scalar_tensor_tensor(out=pn_all[pi][:, 0:192],
                                   in0=p_all[pi][:, 0:192],
                                   scalar=rec_c[pi][:, 0:1],
                                   in1=p_all[pi][:, 0:192],
                                   op0=ALU.mult, op1=ALU.bypass)
            V.scalar_tensor_tensor(out=pn_all[pi][:, 192:192 + NG],
                                   in0=p_all[pi][:, 192:192 + NG],
                                   scalar=rec_c[pi][:, 1:2],
                                   in1=p_all[pi][:, 192:192 + NG],
                                   op0=ALU.mult, op1=ALU.bypass)
            V.scalar_tensor_tensor(out=pn_all[pi][:, GOF:GOF + 192],
                                   in0=p_all[pi][:, GOF:GOF + 192],
                                   scalar=rec_c[pi][:, 2:3],
                                   in1=p_all[pi][:, GOF:GOF + 192],
                                   op0=ALU.mult, op1=ALU.bypass)
            sVp.inc(V.scalar_tensor_tensor(
                out=pn_all[pi][:, GOF + 192:2 * GOF],
                in0=p_all[pi][:, GOF + 192:2 * GOF],
                scalar=rec_c[pi][:, 3:4],
                in1=p_all[pi][:, GOF + 192:2 * GOF],
                op0=ALU.mult, op1=ALU.bypass))

        def emit_tr_av(u):
            p, qq = units[u]
            pi = u % 2
            g = p // 2
            P.wait_ge(sVp.h, u + 1)
            P.wait_ge(sAu.h, max(0, u - 1))
            P.wait_ge(sVav.h, max(0, u - 1))
            P.transpose(pb_bf[4 + pi][0:128, 0:128],
                        pn_all[pi][:, 0:128], ident_bf[:])
            P.matmul(pb_bf[4 + pi][0:128, 128:256],
                     pn_all[pi][:, GOF:GOF + 128], ident_bf[:],
                     is_transpose=True, start=False, stop=True,
                     skip_group_check=True)
            P.transpose(pb_bf[6 + pi][0:64, 0:128],
                        pn_all[pi][:, 128:192], ident_bf[:])
            P.matmul(pb_bf[6 + pi][0:64, 128:256],
                     pn_all[pi][:, GOF + 128:GOF + 192], ident_bf[:],
                     is_transpose=True, start=False, stop=True,
                     skip_group_check=True)
            P.transpose(pb_bf[pi][0:NG, 0:128],
                        pn_all[pi][:, 192:192 + NG], ident_bf[:])
            sPtr.inc(P.matmul(pb_bf[pi][0:NG, 128:256],
                              pn_all[pi][:, GOF + 192:2 * GOF], ident_bf[:],
                              is_transpose=True, start=False, stop=True,
                              skip_group_check=True))
            A.wait_ge(sPtr.h, u + 1)
            A.wait_ge(sPav.h, max(0, u - 1))
            sAt.inc(A.copy(ptgs[pi][0:NG, :], pb_bf[pi][0:NG, 0:256]))
            A.copy(pt1s[pi][:], pb_bf[4 + pi][0:128, 0:256])
            sAu.inc(A.copy(pt2s[pi][:], pb_bf[6 + pi][0:64, 0:256]))
            avb = pb[6 + pi]
            P.wait_ge(sAu.h, u + 1)
            mm = None
            for hh2 in range(2):
                tp = (0, 0) if hh2 == 0 else (0, 64)
                oro = (0, 64) if hh2 == 0 else (64, 128)
                out_av = avb.ap()[oro[0]:oro[1], 0:128]
                P.matmul(out_av,
                         v_tm[:, qq * KVD + g * HD: qq * KVD + (g + 1) * HD],
                         pt1s[pi][:, hh2 * 128:(hh2 + 1) * 128],
                         start=True, stop=False,
                         tile_position=tp, skip_group_check=True)
                P.matmul(out_av,
                         v_tm[0:64, (qq + 1) * KVD + g * HD:
                              (qq + 1) * KVD + (g + 1) * HD],
                         pt2s[pi][0:64, hh2 * 128:(hh2 + 1) * 128],
                         start=False, stop=False,
                         tile_position=tp, skip_group_check=True)
                mm = P.matmul(out_av,
                              vg_tm[:, g * HD:(g + 1) * HD],
                              ptgs[pi][0:NG, hh2 * 128:(hh2 + 1) * 128],
                              start=False, stop=True,
                              tile_position=tp, skip_group_check=True)
            sPav.inc(mm)

        def emit_avcopy(u):
            p, qq = units[u]
            pi = u % 2
            avb = pb[6 + pi]
            V.wait_ge(sPav.h, u + 1)
            sVav.inc(V.tensor_copy(
                attn_fm[:, p * T + qq * 128: p * T + (qq + 1) * 128],
                avb.ap()[:, 0:128]))

        for i in range(NU + 3):
            if i >= 3:
                emit_avcopy(i - 3)
            if 2 <= i < NU + 2:
                emit_stage_b(i - 2)
                emit_tr_av(i - 2)
            if i < NU:
                emit_scores(i)
            if 1 <= i <= NU:
                emit_exps(i - 1)
            pump2()
        while st_state["quant_idx"] < len(jobs) or not pump_state["bcast"]:
            pump2()

        nc.leave_named_scope("PHD", _sc_PHD[0], False)
        _sc_PHE2 = nc.enter_named_scope("PHE2", False)
        nc.leave_named_scope("PHE2", _sc_PHE2[0], False)
        _sc_PHF = nc.enter_named_scope("PHF", False)
        A.wait_ge(cc1s.h, cc1s.n)
        A.wait_ge(ldc.h, ldc.n)
        sAq.inc(A.activation(out=sig_c[:], in_=clipx_sb[:], func=AF.Sigmoid,
                             bias=0.0, scale=1.0))
        V.wait_ge(sAq.h, sAq.n)
        VC(V.tensor_tensor(out=sx_c[:], in0=sig_c[:], in1=gmax_sb[0:1, 0:1],
                           op=ALU.mult))
        VC(V.tensor_scalar(out=sx_c[:], in0=sx_c[:], scalar1=1.0 / QMAX,
                           scalar2=None, op0=ALU.mult))
        sVq2.inc(V.reciprocal(rsx_c[:], sx_c[:]))
        G.wait_ge(sVq2.h, sVq2.n)
        gsc.inc(G.dma_start(out=sc1_s[0:1, 0:1], in_=rsx_c[:]), 16)
        gsc.inc(G.dma_start(out=sc1_s[0:1, 1:2], in_=sx_c[:]), 16)
        S.wait_ge(gsc.h, gsc.n)
        ssc.inc(S.dma_start(out=bc1[:, 0:2],
                            in_=sc1_s[0:1, 0:2].partition_broadcast(128)), 16)
        V.wait_ge(ssc.h, ssc.n)
        V.wait_ge(sVw.h, sVw.n)
        V.scalar_tensor_tensor(out=sug_all[:], in0=swg_all[:],
                               scalar=bc1[:, 1:2], in1=swg_all[:],
                               op0=ALU.mult, op1=ALU.bypass)
        sVq2.inc(V.scalar_tensor_tensor(out=suu_all[:], in0=swu_all[:],
                                        scalar=bc1[:, 1:2], in1=swu_all[:],
                                        op0=ALU.mult, op1=ALU.bypass))

        nc.leave_named_scope("PHF", _sc_PHF[0], False)
        s2.close()
        scope_barrier(stw, stwb, gsc, sVav, sAe, sVp)

        sxint = ExitStack()
        Xint = sxint.enter_context(
            nc.sbuf_tensor("Xint", [128, FCH * XP], BF16, side="right"))

        s3 = ExitStack()
        sb3 = lambda name, shape, dt: s3.enter_context(
            nc.sbuf_tensor(name, shape, dt))
        xt2 = [sb3(f"xt2_{i}", [128, D], F32) for i in range(2)]
        xq_t = sb3("xq_t", [128, D], F32)
        xq_i = sb3("xq_i", [128, D], I32)
        xq_b = sb3("xq_b", [128, D], BF16)
        wo_sb = sb3("wo_sb", [128, FCH * D], BF16)
        xres = [sb3(f"xres{i}", [128, D], F32) for i in range(2)]
        resid = [sb3(f"resid{i}", [128, D], F32) for i in range(2)]

        _sc_PHG1 = nc.enter_named_scope("PHG1", False)
        def g1_load(j, gate=None):
            if j >= NTT:
                return
            r2 = tile_rows(j)
            buf2 = j % 2
            if gate is not None:
                S.wait_ge(sAq.h, gate)
            for q4 in range(0, D, 256):
                [lxa, lxb][buf2].inc(
                    S.dma_start(out=xt2[buf2][0:r2, q4:q4 + 256],
                                in_=x_ext[j * 128:j * 128 + r2,
                                          q4:q4 + 256]), 16)

        g1_load(0)
        g1_load(1)
        for j in range(NTT):
            r = tile_rows(j)
            buf = j % 2
            V.wait_ge([lxa, lxb][buf].h, [lxa, lxb][buf].n)
            V.wait_ge(sAq.h, sAq.n)
            V.scalar_tensor_tensor(out=rstd2[0:r, :],
                                   in0=rstd_sl[0:r, j:j + 1],
                                   scalar=bc1[0:r, 0:1],
                                   in1=rstd_sl[0:r, j:j + 1],
                                   op0=ALU.mult, op1=ALU.bypass)
            sVq2.inc(V.scalar_tensor_tensor(out=nmrs2[0:r, :],
                                            in0=nmrs_sl[0:r, j:j + 1],
                                            scalar=bc1[0:r, 0:1],
                                            in1=nmrs_sl[0:r, j:j + 1],
                                            op0=ALU.mult, op1=ALU.bypass))
            A.wait_ge(sVq2.h, sVq2.n)
            sAq.inc(A.activation(out=xq_t[0:r, :], in_=xt2[buf][0:r, :],
                                 func=AF.Identity, bias=nmrs2[0:r, 0:1],
                                 scale=rstd2[0:r, 0:1]))
            g1_load(j + 2, gate=sAq.n)
            V.wait_ge(sAq.h, sAq.n)
            VC(V.tensor_copy(xq_i[0:r, :], xq_t[0:r, :]))
            V.wait_ge(stq.h, stq.n)
            sVq2.inc(V.tensor_scalar(out=xq_b[0:r, :], in0=xq_i[0:r, :],
                                     scalar1=QMAX, scalar2=-QMAX - 1.0,
                                     op0=ALU.min, op1=ALU.max))
            S.wait_ge(sVq2.h, sVq2.n)
            stq.inc(S.dma_start(out=xq_s[j * 128:j * 128 + r, :],
                                in_=xq_b[0:r, :]), 16)
        V.wait_ge(stq.h, stq.n)
        sVq2.inc(V.memset(xq_b[0:64, :], 0.0))
        S.wait_ge(sVq2.h, sVq2.n)
        stq.inc(S.dma_start(out=xq_s[T + 64:XP, :], in_=xq_b[0:64, :]), 16)
        S.wait_ge(stq.h, stq.n)
        for f in range(FCH):
            trx.inc(S.dma_start(out=Xint[:, f * XP:(f + 1) * XP],
                                in_=xq_s[:, f * 128:(f + 1) * 128],
                                transpose=True), 16)

        nc.leave_named_scope("PHG1", _sc_PHG1[0], False)
        _sc_PHG2 = nc.enter_named_scope("PHG2", False)
        for f in range(FCH):
            ldwa.inc(G.dma_start(out=wo_sb[:, f * D:(f + 1) * D],
                                 in_=Wo[f * 128:(f + 1) * 128, :]), 16)
        P.wait_ge(ldwa.h, ldwa.n)
        P.wait_ge(sVav.h, sVav.n)
        def g2_load(tt, gate=None):
            if tt >= T // 128:
                return
            buf2 = tt % 2
            if gate is not None:
                S.wait_ge(sVo.h, gate)
            for q4 in range(0, D, 256):
                [lxa, lxb][buf2].inc(
                    S.dma_start(out=xres[buf2][:, q4:q4 + 256],
                                in_=x_ext[64 + tt * 128: 64 + (tt + 1) * 128,
                                          q4:q4 + 256]), 16)

        g2_load(0)
        g2_load(1)
        for tt in range(T // 128):
            buf = tt % 2
            for m2 in range(2):
                bank = pb[m2]
                P.wait_ge(sVo.h, max(0, sVo.n - 1))
                mm = None
                for f in range(FCH):
                    mm = P.matmul(
                        bank.ap()[:, 0:512],
                        attn_fm[:, f * T + tt * 128: f * T + (tt + 1) * 128],
                        wo_sb[:, f * D + m2 * 512: f * D + (m2 + 1) * 512],
                        start=(f == 0), stop=(f == FCH - 1))
                sPo.inc(mm)
                V.wait_ge(sPo.h, sPo.n)
                V.wait_ge([lxa, lxb][buf].h, [lxa, lxb][buf].n)
                if tt >= 2 and m2 == 0:
                    V.wait_ge(sty.h, sty.n)
                sVo.inc(V.tensor_tensor(
                    out=resid[buf][:, m2 * 512:(m2 + 1) * 512],
                    in0=bank.ap()[:, 0:512],
                    in1=xres[buf][:, m2 * 512:(m2 + 1) * 512], op=ALU.add))
            g2_load(tt + 2, gate=sVo.n)
            S.wait_ge(sVo.h, sVo.n)
            sty.inc(S.dma_start(out=y[tt * 128:(tt + 1) * 128, :],
                                in_=resid[buf][:]), 16)

        nc.leave_named_scope("PHG2", _sc_PHG2[0], False)
        s3.close()
        sattn.close()
        scope_barrier(sty, stq, trx, sVo, sVq2, sAq)

        s4 = ExitStack()
        sb4 = lambda name, shape, dt: s4.enter_context(
            nc.sbuf_tensor(name, shape, dt))
        assert n_spill <= n_keep
        h_sb = sb4("h_sb", [128, max(1, n_keep) * T], F32)
        hsp_buf = [sb4(f"hsp{i}", [128, T], F32) for i in range(2)]
        hint_view = h_sb.ap().bitcast(BF16)

        s5 = ExitStack()
        sb5 = lambda name, shape, dt: s5.enter_context(
            nc.sbuf_tensor(name, shape, dt))
        wstr = [sb5(f"wstr{i}", [128, FCH * 512], BF16) for i in range(3)]
        silu_sb = sb5("silu_sb", [128, 512], F32)

        _sc_PHH = nc.enter_named_scope("PHH", False)
        groups = []
        c0 = 0
        while c0 < NCHUNK:
            groups.append(list(range(c0, min(c0 + 4, NCHUNK))))
            c0 += 4
        cum_chunks = [0]
        for grp in groups:
            cum_chunks.append(cum_chunks[-1] + len(grp))

        S.wait_ge(stw.h, stw.n)
        S.wait_ge(stwb.h, stwb.n)
        A.wait_ge(sVq2.h, sVq2.n)

        def emit_witem(k):
            gi, kind = k // 2, k % 2
            grp = groups[gi]
            iw = len(grp) * 128
            srcw = wgq_s if kind == 0 else wuq_s
            if k >= 3:
                sem_p = sPf if (k - 3) % 2 == 0 else sPu
                S.wait_ge(sem_p.h, cum_chunks[(k - 3) // 2 + 1] * T4)
            sl = trwl[k % 3]
            for f in range(FCH):
                sl.inc(S.dma_start(
                    out=wstr[k % 3][:, f * 512: f * 512 + iw],
                    in_=srcw[grp[0] * 128: grp[0] * 128 + iw,
                             f * 128:(f + 1) * 128], transpose=True), 16)
            witem_done.append((sl, sl.n))

        witem_done = []
        sp_i = 0
        emit_witem(0)
        emit_witem(1)
        for gi, grp in enumerate(groups):
            if 2 * gi + 2 < 2 * len(groups):
                emit_witem(2 * gi + 2)
            if 2 * gi + 3 < 2 * len(groups):
                emit_witem(2 * gi + 3)
            gslot = (2 * gi) % 3
            uslot = (2 * gi + 1) % 3
            if gi == 0:
                P.wait_ge(trx.h, trx.n)
            for ici, ic in enumerate(grp):
                il = ici * 128
                if ici == 0:
                    P.wait_ge(witem_done[2 * gi][0].h, witem_done[2 * gi][1])
                cno = cum_chunks[gi] + ici
                for t in range(T4):
                    P.wait_ge(sAf.h, max(0, (cno - 1) * T4 + t + 1))
                    mm = None
                    for f in range(FCH):
                        mm = P.matmul(
                            pb[t].ap()[:, 0:512],
                            wstr[gslot][:, f * 512 + il: f * 512 + il + 128],
                            Xint[:, f * XP + 64 + t * 512:
                                 f * XP + 64 + (t + 1) * 512],
                            start=(f == 0), stop=(f == FCH - 1))
                    sPf.inc(mm)
                if ici == 0:
                    P.wait_ge(witem_done[2 * gi + 1][0].h,
                              witem_done[2 * gi + 1][1])
                for t in range(T4):
                    P.wait_ge(sVh.h, max(0, (cno - 1) * T4 + t + 1))
                    mm = None
                    for f in range(FCH):
                        mm = P.matmul(
                            pb[4 + t].ap()[:, 0:512],
                            wstr[uslot][:, f * 512 + il: f * 512 + il + 128],
                            Xint[:, f * XP + 64 + t * 512:
                                 f * XP + 64 + (t + 1) * 512],
                            start=(f == 0), stop=(f == FCH - 1))
                    sPu.inc(mm)
                spill = ic >= n_keep
                for t in range(T4):
                    A.wait_ge(sPf.h, cno * T4 + t + 1)
                    A.wait_ge(sVh.h, max(0, cno * T4 + t))
                    sAf.inc(A.activation(out=silu_sb[:],
                                         in_=pb[t].ap()[:, 0:512],
                                         func=AF.Silu, bias=0.0,
                                         scale=sug_all[:, ic:ic + 1]))
                    V.wait_ge(sAf.h, cno * T4 + t + 1)
                    V.wait_ge(sPu.h, cno * T4 + t + 1)
                    if spill:
                        if sp_i >= 2:
                            V.wait_ge(gsp.h, gsp.n)
                        hdst = hsp_buf[sp_i % 2][:, t * 512:(t + 1) * 512]
                    else:
                        hdst = h_sb[:, ic * T + t * 512: ic * T + (t + 1) * 512]
                    sVh.inc(V.scalar_tensor_tensor(out=hdst,
                                               in0=pb[4 + t].ap()[:, 0:512],
                                               scalar=suu_all[:, ic:ic + 1],
                                               in1=silu_sb[:], op0=ALU.mult,
                                               op1=ALU.mult))
                    V.wait_ge(sVh.h, sVh.n)
                    VC(V.tensor_reduce(out=amx_c[:], in_=hdst, axis=AX.X,
                                       op=ALU.max, apply_absolute_value=True))
                    V.wait_ge(sVf.h, sVf.n)
                    sVf.inc(V.tensor_tensor(out=hacc[:], in0=hacc[:],
                                            in1=amx_c[:], op=ALU.max))
                if spill:
                    G.wait_ge(sVf.h, sVf.n)
                    gsp.inc(G.dma_start(
                        out=h_sp[(ic - n_keep) * 128:
                                 (ic - n_keep + 1) * 128, :],
                        in_=hsp_buf[sp_i % 2][:]), 16)
                    sp_i += 1

        nc.leave_named_scope("PHH", _sc_PHH[0], False)
        s5.close()
        sxint.close()
        scope_barrier(sPf, sAf, sVf, gsp)

        s6 = ExitStack()
        sb6 = lambda name, shape, dt: s6.enter_context(
            nc.sbuf_tensor(name, shape, dt))
        wdT_sb = sb6("wdT_sb", [128, NCHUNK * 512], BF16)
        swd_row = sb6("swd_row", [128, D], F32)
        hq_t = sb6("hq_t", [128, 1024], F32)
        hq_i = sb6("hq_i", [128, 1024], I32)
        dout = [sb6(f"dout{i}", [128, 512], F32) for i in range(2)]

        _sc_PHI = nc.enter_named_scope("PHI", False)
        P.wait_ge(sVf.h, sVf.n)
        sPt.inc(P.transpose(pb[2].ap()[0:1, 0:128], hacc[:], ident_f[:]))
        V.wait_ge(sPt.h, sPt.n)
        sVc.inc(V.tensor_reduce(out=red2[0:1, 0:1],
                                in_=pb[2].ap()[0:1, 0:128], axis=AX.X,
                                op=ALU.max))
        G.wait_ge(gsp.h, gsp.n)
        G.wait_ge(sVc.h, sVc.n)
        cc2s.inc(G.dma_start(out=cc2_in, in_=red2[:]), 16)
        G.wait_ge(cc2s.h, cc2s.n)
        cc2c.inc(G.collective_compute(
            "AllReduce", ALU.max, replica_groups=[list(range(NCORES))],
            ins=[cc2_in], outs=[cc2_out]))
        G.wait_ge(cc2c.h, cc2c.n)
        cc2s.inc(G.dma_start(out=hmax_sb[:], in_=cc2_out), 16)

        def load_wdT_half(m2):
            for ic in range(NCHUNK):
                trd.inc(S.dma_start(
                    out=wdT_sb[:, ic * 512:(ic + 1) * 512],
                    in_=wdqT_s[ic * 128:(ic + 1) * 128,
                               m2 * 512:(m2 + 1) * 512]), 16)

        S.wait_ge(sty.h, sty.n)
        load_wdT_half(0)
        ssc.inc(S.dma_start(out=swd_row[:],
                            in_=swd_s.partition_broadcast(128)), 16)

        A.wait_ge(cc2s.h, cc2s.n)
        sAj.inc(A.activation(out=sig_c[:], in_=clipd_sb[:], func=AF.Sigmoid,
                             bias=0.0, scale=1.0))
        V.wait_ge(sAj.h, sAj.n)
        VC(V.tensor_tensor(out=sh_c[:], in0=sig_c[:], in1=hmax_sb[0:1, 0:1],
                           op=ALU.mult))
        VC(V.tensor_scalar(out=sh_c[:], in0=sh_c[:], scalar1=1.0 / QMAX,
                           scalar2=None, op0=ALU.mult))
        sVj.inc(V.reciprocal(rsh_c[:], sh_c[:]))
        G.wait_ge(sVj.h, sVj.n)
        gsc.inc(G.dma_start(out=sc2_s[0:1, 0:1], in_=rsh_c[:]), 16)
        gsc.inc(G.dma_start(out=sc2_s[0:1, 1:2], in_=sh_c[:]), 16)
        S.wait_ge(gsc.h, gsc.n)
        ssc.inc(S.dma_start(out=bc2[:, 0:2],
                            in_=sc2_s[0:1, 0:2].partition_broadcast(128)), 16)

        nc.leave_named_scope("PHI", _sc_PHI[0], False)
        _sc_PHJ = nc.enter_named_scope("PHJ", False)
        def hint_ap(ic, lo, wlen):
            if ic < n_keep:
                base = 2 * ic * T
            else:
                base = (2 * (ic - n_keep) + 1) * T
            return hint_view[:, base + lo: base + lo + wlen]

        A.wait_ge(ssc.h, ssc.n)
        rl_i = 0
        for ic in range(NCHUNK):
            if ic >= n_keep:
                gbuf = rl_i % 2
                G.wait_ge(sVj.h, sVj.n)
                grl.inc(G.dma_start(
                    out=hsp_buf[gbuf][:],
                    in_=h_sp[(ic - n_keep) * 128:(ic - n_keep + 1) * 128, :]),
                    16)
                src_t, src_off = hsp_buf[gbuf], 0
                rl_i += 1
            else:
                src_t, src_off = h_sb, ic * T
            for lo in range(0, T, 1024):
                w = min(1024, T - lo)
                if ic >= n_keep:
                    A.wait_ge(grl.h, grl.n)
                A.wait_ge(sVx.h, sVx.n)
                sAj.inc(A.activation(
                    out=hq_t[:, 0:w], in_=src_t[:, src_off + lo: src_off + lo + w],
                    func=AF.Identity, bias=0.0, scale=bc2[:, 0:1]))
                V.wait_ge(sAj.h, sAj.n)
                V.wait_ge(sVj.h, sVj.n)
                VC(V.tensor_copy(hq_i[:, 0:w], hq_t[:, 0:w]))
                sVj.inc(V.tensor_scalar(out=hint_ap(ic, lo, w), in0=hq_i[:, 0:w],
                                        scalar1=QMAX, scalar2=-QMAX - 1.0,
                                        op0=ALU.min, op1=ALU.max))

        nc.leave_named_scope("PHJ", _sc_PHJ[0], False)
        _sc_PHK = nc.enter_named_scope("PHK", False)
        P.wait_ge(sVj.h, sVj.n)
        G.wait_ge(sty.h, sty.n)
        idx = 0
        for m2 in range(2):
            if m2 == 1:
                S.wait_ge(sPd.h, sPd.n)
                load_wdT_half(1)
            P.wait_ge(trd.h, trd.n)
            for tt in range(T // 128):
                bank = pb[idx % 2]
                P.wait_ge(sVe.h, max(0, sVe.n - 1))
                mm = None
                for ic in range(NCHUNK):
                    mm = P.matmul(bank.ap()[:, 0:512],
                                  hint_ap(ic, tt * 128, 128),
                                  wdT_sb[:, ic * 512:(ic + 1) * 512],
                                  start=(ic == 0), stop=(ic == NCHUNK - 1))
                sPd.inc(mm)
                V.wait_ge(sPd.h, sPd.n)
                if idx >= 2:
                    V.wait_ge(acc.h, acc.n)
                dbuf = dout[idx % 2]
                sVe.inc(V.scalar_tensor_tensor(
                    out=dbuf[:], in0=bank.ap()[:, 0:512],
                    scalar=bc2[:, 1:2],
                    in1=swd_row[:, m2 * 512:(m2 + 1) * 512],
                    op0=ALU.mult, op1=ALU.mult))
                G.wait_ge(sVe.h, sVe.n)
                acc.inc(G.dma_start(
                    out=y[tt * 128:(tt + 1) * 128, m2 * 512:(m2 + 1) * 512],
                    in_=dbuf[:], accum_op=ALU.add), 16)
                idx += 1

        nc.leave_named_scope("PHK", _sc_PHK[0], False)
        G.wait_ge(acc.h, acc.n)
        G.wait_ge(sty.h, sty.n)

        s6.close()
        s4.close()

    return nc


def _alibi_slopes(n_heads):
    start = 2.0 ** (-2.0 ** (-(math.log2(n_heads) - 3)))
    return np.array([start * start ** i for i in range(n_heads)],
                    dtype=np.float32)


def _host_prep(x, Wq, Wk, Wv, Wo, clip_x, clip_d, wgm, wum, wdm, T):
    B, N, C = x.shape
    slopes = _alibi_slopes(HEADS)
    i_idx = np.arange(128)[:, None]
    j_idx = np.arange(192)[None, :]
    off = (j_idx - 64 - i_idx).astype(np.float32)
    allowed = (j_idx >= i_idx) & (j_idx <= i_idx + 64)
    mask_all = np.where(allowed[None],
                        8.0 * slopes[:, None, None] * off[None], NEG)
    mask_all = np.ascontiguousarray(mask_all.astype(np.float32))
    mask_kill = mask_all.copy()
    mask_kill[:, :, 0:64] = NEG
    wdm = np.asarray(wdm, np.float32)

    common = {
        "Wq": np.ascontiguousarray(Wq, dtype=np.float32),
        "Wk": np.ascontiguousarray(Wk, dtype=np.float32),
        "Wv": np.ascontiguousarray(Wv, dtype=np.float32),
        "Wo": np.ascontiguousarray(Wo, dtype=np.float32),
        "wg": np.ascontiguousarray(wgm, dtype=np.float32),
        "wu": np.ascontiguousarray(wum, dtype=np.float32),
        "wd": np.ascontiguousarray(wdm),
        "wdT": np.ascontiguousarray(wdm.T),
        "clip_x": np.asarray(clip_x, np.float32).reshape(1, 1),
        "clip_d": np.asarray(clip_d, np.float32).reshape(1, 1),
        "masks": mask_all,
    }
    in_maps = []
    for c in range(NCORES):
        b, hh = c // 2, c % 2
        s0 = hh * T
        if hh == 0:
            halo = np.zeros((64, C), np.float32)
        else:
            halo = np.asarray(x[b, s0 - 64:s0, :], np.float32)
        x_core = np.concatenate(
            [halo, np.asarray(x[b, s0:s0 + T, :], np.float32)], axis=0)
        m = dict(common)
        m["x_ext"] = np.ascontiguousarray(x_core)
        m["x_glob"] = np.ascontiguousarray(np.asarray(x[b, ::WIN, :],
                                                      np.float32))
        m["masks_q0"] = mask_kill if hh == 0 else mask_all
        in_maps.append(m)
    return in_maps


def kernel(L0_attn_x, L0_attn_Wq, L0_attn_Wk, L0_attn_Wv, L0_attn_Wo,
           L0_ffn_clip_x, L0_ffn_clip_down, L0_ffn_w_gate, L0_ffn_w_up,
           L0_ffn_w_down):
    T = 2048
    x = np.asarray(L0_attn_x, np.float32)
    B, N, C = x.shape
    in_maps = _host_prep(x, L0_attn_Wq, L0_attn_Wk, L0_attn_Wv, L0_attn_Wo,
                         L0_ffn_clip_x, L0_ffn_clip_down, L0_ffn_w_gate,
                         L0_ffn_w_up, L0_ffn_w_down, T)
    nc = build_program(T=T)
    res = run_bass_kernel_spmd(nc, in_maps, core_ids=list(range(NCORES)))
    out = np.empty((B, N, C), np.float32)
    for c in range(NCORES):
        b, hh = c // 2, c % 2
        out[b, hh * T:(hh + 1) * T, :] = res.results[c]["y"]
    return out


# version 21
# speedup vs baseline: 1.0032x; 1.0032x over previous
import math
import numpy as np
import ml_dtypes
from contextlib import ExitStack

import concourse.bass as bass
import concourse.mybir as mybir
from concourse.bass_utils import run_bass_kernel_spmd

F32 = mybir.dt.float32
BF16 = mybir.dt.bfloat16
I32 = mybir.dt.int32
AF = mybir.ActivationFunctionType
ALU = mybir.AluOpType
AX = mybir.AxisListType

D = 1024
HEADS = 16
KVH = 4
HD = 64
KVD = 256
WIN = 64
IDIM = 2816
NCHUNK = IDIM // 128
FCH = D // 128
NCORES = 8
EPS = 1e-5
SCALE = HD ** -0.5
QMAX = 127.0
NEG = -1.0e30


class Sem:

    def __init__(self, ctx, nc, name):
        self.h = ctx.enter_context(nc.semaphore(name))
        self.n = 0

    def inc(self, instr, k=1):
        instr.then_inc(self.h, k)
        self.n += k
        return instr


def build_program(T=2048, n_spill=6):
    assert T % 512 == 0
    NT = (T + 64) // 128
    NTT = NT + 1
    QQ = T // 128
    T4 = T // 512
    KT = T + 64
    XP = T + 128
    NG = (2 * T) // WIN
    n_keep = NCHUNK - n_spill

    nc = bass.Bass("TRN2", target_bir_lowering=False, debug=False)

    x_ext = nc.dram_tensor("x_ext", [KT, D], F32, kind="ExternalInput").ap()
    x_glob = nc.dram_tensor("x_glob", [NG, D], F32, kind="ExternalInput").ap()
    Wq = nc.dram_tensor("Wq", [D, D], F32, kind="ExternalInput").ap()
    Wk = nc.dram_tensor("Wk", [D, KVD], F32, kind="ExternalInput").ap()
    Wv = nc.dram_tensor("Wv", [D, KVD], F32, kind="ExternalInput").ap()
    Wo = nc.dram_tensor("Wo", [D, D], F32, kind="ExternalInput").ap()
    wg = nc.dram_tensor("wg", [IDIM, D], F32, kind="ExternalInput").ap()
    wu = nc.dram_tensor("wu", [IDIM, D], F32, kind="ExternalInput").ap()
    wd = nc.dram_tensor("wd", [D, IDIM], F32, kind="ExternalInput").ap()
    wdT = nc.dram_tensor("wdT", [IDIM, D], F32, kind="ExternalInput").ap()
    clip_x = nc.dram_tensor("clip_x", [1, 1], F32, kind="ExternalInput").ap()
    clip_d = nc.dram_tensor("clip_d", [1, 1], F32, kind="ExternalInput").ap()
    masks = nc.dram_tensor("masks", [HEADS, 128, 192], F32,
                           kind="ExternalInput").ap()
    masks_q0 = nc.dram_tensor("masks_q0", [HEADS, 128, 192], F32,
                              kind="ExternalInput").ap()
    y = nc.dram_tensor("y", [T, D], F32, kind="ExternalOutput").ap()

    wgq_s = nc.dram_tensor("wgq_s", [IDIM, D], BF16, kind="Internal").ap()
    wuq_s = nc.dram_tensor("wuq_s", [IDIM, D], BF16, kind="Internal").ap()
    wdqT_s = nc.dram_tensor("wdqT_s", [IDIM, D], BF16, kind="Internal").ap()
    xq_s = nc.dram_tensor("xq_s", [XP, D], BF16, kind="Internal").ap()
    h_sp = nc.dram_tensor("h_sp", [max(1, n_spill) * 128, T], F32,
                          kind="Internal").ap()
    swd_s = nc.dram_tensor("swd_s", [1, D], F32, kind="Internal").ap()
    swdr_s = nc.dram_tensor("swdr_s", [1, D], F32, kind="Internal").ap()
    sc1_s = nc.dram_tensor("sc1_s", [1, 8], F32, kind="Internal").ap()
    sc2_s = nc.dram_tensor("sc2_s", [1, 8], F32, kind="Internal").ap()
    cc1_in = nc.dram_tensor("cc1_in", [1, 8], F32, kind="Internal").ap()
    cc1_out = nc.dram_tensor("cc1_out", [1, 8], F32, kind="Internal").ap()
    cc2_in = nc.dram_tensor("cc2_in", [1, 8], F32, kind="Internal").ap()
    cc2_out = nc.dram_tensor("cc2_out", [1, 8], F32, kind="Internal").ap()

    ident_bf_d = nc.inline_tensor(np.eye(128, dtype=ml_dtypes.bfloat16),
                                  name="ident_bf_c").ap()
    ident_f_d = nc.inline_tensor(np.eye(128, dtype=np.float32),
                                 name="ident_f_c").ap()

    S, A, V, P, G = nc.sync, nc.scalar, nc.vector, nc.tensor, nc.gpsimd

    ctx = ExitStack()
    with ctx:
        def sb(name, shape, dt, side=None):
            return ctx.enter_context(nc.sbuf_tensor(name, shape, dt,
                                                    side=side))

        def psb(name):
            return ctx.enter_context(nc.psum_tensor(name, [128, 512], F32))

        ident_bf = sb("ident_bf", [128, 128], BF16)
        ident_f = sb("ident_f", [128, 128], F32)
        c0125 = sb("c0125", [128, 1], F32)
        epsc = sb("epsc", [128, 1], F32)
        clipx_sb = sb("clipx_sb", [1, 1], F32)
        clipd_sb = sb("clipd_sb", [1, 1], F32)
        rstd_sl = sb("rstd_sl", [128, NTT + 1], F32)
        nmrs_sl = sb("nmrs_sl", [128, NTT + 1], F32)
        rstd2 = sb("rstd2", [128, 1], F32)
        nmrs2 = sb("nmrs2", [128, 1], F32)
        stats6 = sb("stats6", [128, 12], F32)
        mvc = sb("mvc", [128, 2], F32)
        sd_c = sb("sd_c", [128, 1], F32)
        gacc = sb("gacc", [128, 1], F32)
        hacc = sb("hacc", [128, 1], F32)
        amx_c = sb("amx_c", [128, 1], F32)
        red1 = sb("red1", [1, 8], F32)
        red2 = sb("red2", [1, 8], F32)
        gmax_sb = sb("gmax_sb", [1, 8], F32)
        hmax_sb = sb("hmax_sb", [1, 8], F32)
        sig_c = sb("sig_c", [1, 1], F32)
        sx_c = sb("sx_c", [1, 1], F32)
        rsx_c = sb("rsx_c", [1, 1], F32)
        sh_c = sb("sh_c", [1, 1], F32)
        rsh_c = sb("rsh_c", [1, 1], F32)
        bc1 = sb("bc1", [128, 8], F32)
        bc2 = sb("bc2", [128, 8], F32)
        swg_all = sb("swg_all", [128, NCHUNK], F32)
        swu_all = sb("swu_all", [128, NCHUNK], F32)
        sug_all = sb("sug_all", [128, NCHUNK], F32)
        suu_all = sb("suu_all", [128, NCHUNK], F32)
        wamax = sb("wamax", [128, 1], F32)
        wam2 = sb("wam2", [128, 1], F32)
        wrec = sb("wrec", [128, 1], F32)
        wsc127 = sb("wsc127", [128, 1], F32)

        pb = [psb(f"pb{i}") for i in range(8)]
        pb_bf = [b.ap().bitcast(BF16) for b in pb]

        names = ("ld0 ld1 ldc ldwa ldff0 ldff1 ldff2 sVa sAa sVb sAb sVc sPt sVd "
                 "sPq sVq gdup cc1s cc2s cc1c cc2c sPs sVs sAe sVp sPtr sAt sPav sVav "
                 "sVw sAw stw sAq sVq2 stq trx sPo sVo sty trw sPf sAf sVf "
                 "gsp trd grl sAj sVj sPd sVe acc gsc ssc lxa lxb ldi sVx sAx trw1 trw2 stwb sVh sPu sAu").split()
        sm = {n: Sem(ctx, nc, n) for n in names}
        (ld0, ld1, ldc, ldwa, ldff0, ldff1, ldff2, sVa, sAa, sVb, sAb, sVc, sPt,
         sVd, sPq, sVq, gdup, cc1s, cc2s, cc1c, cc2c, sPs, sVs, sAe, sVp, sPtr, sAt,
         sPav, sVav, sVw, sAw, stw, sAq, sVq2, stq, trx, sPo, sVo, sty,
         trw, sPf, sAf, sVf, gsp, trd, grl, sAj, sVj, sPd, sVe, acc, gsc,
         ssc, lxa, lxb, ldi, sVx, sAx, trw1, trw2, stwb, sVh, sPu, sAu) = (sm[n] for n in names)
        trwl = [trw, trw1, trw2]

        def VC(instr):
            sVx.inc(instr)
            V.wait_ge(sVx.h, sVx.n)
            return instr

        def AC(instr):
            sAx.inc(instr)
            A.wait_ge(sAx.h, sAx.n)
            return instr

        def scope_barrier(*sems):
            for _eng in (V, A, S, P, G):
                for _s in sems:
                    _eng.wait_ge(_s.h, _s.n)

        ldi.inc(S.dma_start(out=ident_bf[:], in_=ident_bf_d), 16)
        ldi.inc(S.dma_start(out=ident_f[:], in_=ident_f_d), 16)
        ldc.inc(S.dma_start(out=clipx_sb[:], in_=clip_x), 16)
        ldc.inc(S.dma_start(out=clipd_sb[:], in_=clip_d), 16)
        V.memset(c0125[:], SCALE)
        V.memset(epsc[:], EPS)
        V.memset(gacc[:], 0.0)
        V.memset(hacc[:], 0.0)
        V.memset(red1[:], 0.0)
        V.memset(red2[:], 0.0)

        sattn = ExitStack()
        attn_fm = sattn.enter_context(
            nc.sbuf_tensor("attn_fm", [128, FCH * T], BF16))

        s2 = ExitStack()
        sb2 = lambda name, shape, dt: s2.enter_context(
            nc.sbuf_tensor(name, shape, dt))
        sb2r = lambda name, shape, dt: s2.enter_context(
            nc.sbuf_tensor(name, shape, dt, side="right"))
        q_fm = sb2("q_fm", [128, FCH * T], BF16)
        k_dup = sb2("k_dup", [128, KVH * KT], BF16)
        kg_dup = sb2r("kg_dup", [128, KVH * NG], BF16)
        v_tm = sb2("v_tm", [128, NTT * KVD], BF16)
        vg_tm = sb2r("vg_tm", [max(NG, 1), KVD], BF16)
        masks_sb = sb2r("masks_sb", [128, HEADS * 192], BF16)
        masksq0_sb = sb2r("masksq0_sb", [128, HEADS * 192], BF16)
        p_all = [sb2r(f"p_all{i}", [128, 512], BF16) for i in range(2)]
        pn_all = [sb2r(f"pn_all{i}", [128, 512], BF16) for i in range(2)]
        pt1s = [sb2r(f"pt1s{i}", [128, 256], BF16) for i in range(2)]
        pt2s = [sb2r(f"pt2s{i}", [64, 256], BF16) for i in range(2)]
        ptgs = [sb2r(f"ptgs{i}", [64, 256], BF16) for i in range(2)]
        mx_c = [sb2(f"mx_c{i}", [128, 2], F32) for i in range(2)]
        nmx_c = [sb2(f"nmx_c{i}", [128, 2], F32) for i in range(2)]
        sums_c = [sb2(f"sums_c{i}", [128, 4], F32) for i in range(2)]
        rec_c = [sb2(f"rec_c{i}", [128, 4], F32) for i in range(2)]
        wbuf = [sb2r(f"wbuf{i}", [128, 1024], F32) for i in range(3)]
        wi_s = [sb2r(f"wi_s{i}", [128, 256], I32) for i in range(2)]
        wqb_s = [sb2r(f"wqb_s{i}", [128, 256], BF16) for i in range(2)]
        swdr_row = sb2r("swdr_row", [128, D], F32)

        s1 = ExitStack()
        sb1 = lambda name, shape, dt: s1.enter_context(
            nc.sbuf_tensor(name, shape, dt))
        xtile = [sb1(f"xtile{i}", [128, D], F32) for i in range(2)]
        xnf = sb1("xnf", [128, D], F32)
        xnb = [sb1(f"xnb{i}", [128, D], BF16) for i in range(2)]
        xn_fm = sb1("xn_fm", [128, FCH * KT], BF16)
        xng_fm = sb1("xng_fm", [128, FCH * NG], BF16)
        wq_sb = sb1("wq_sb", [128, FCH * D], BF16)
        wk_sb = sb1("wk_sb", [128, FCH * KVD], BF16)
        wv_sb = sb1("wv_sb", [128, FCH * KVD], BF16)

        for f in range(FCH):
            ldwa.inc(G.dma_start(out=wq_sb[:, f * D:(f + 1) * D],
                                 in_=Wq[f * 128:(f + 1) * 128, :]), 16)
            ldwa.inc(G.dma_start(out=wk_sb[:, f * KVD:(f + 1) * KVD],
                                 in_=Wk[f * 128:(f + 1) * 128, :]), 16)
            ldwa.inc(G.dma_start(out=wv_sb[:, f * KVD:(f + 1) * KVD],
                                 in_=Wv[f * 128:(f + 1) * 128, :]), 16)
        for hh in range(HEADS):
            ldc.inc(G.dma_start(
                out=masks_sb[:, hh * 192:(hh + 1) * 192],
                in_=masks[hh]), 16)
            ldc.inc(G.dma_start(
                out=masksq0_sb[:, hh * 192:(hh + 1) * 192],
                in_=masks_q0[hh]), 16)

        _sc_PHA = nc.enter_named_scope("PHA", False)
        def tile_rows(j):
            return 128 if j < NT else 64

        ld_par = [ld0, ld1]

        def pha_load(j, gate=None):
            if j > NTT:
                return
            r2 = tile_rows(j) if j <= NT else NG
            buf2 = j % 2
            if gate is not None:
                S.wait_ge(sAb.h, gate)
            src2 = x_ext[j * 128: j * 128 + r2, :] if j <= NT else x_glob
            for q4 in range(0, D, 256):
                ld_par[buf2].inc(S.dma_start(
                    out=xtile[buf2][0:r2, q4:q4 + 256],
                    in_=src2[:, q4:q4 + 256]), 16)

        pha_load(0)
        pha_load(1)
        n_amax = 0
        for j in range(NTT + 1):
            r = tile_rows(j) if j <= NT else NG
            buf = j % 2
            V.wait_ge(ld_par[buf].h, ld_par[buf].n)
            V.bn_stats(stats6[0:r, 0:6], xtile[buf][0:r, 0:512])
            VC(V.bn_stats(stats6[0:r, 6:12], xtile[buf][0:r, 512:1024]))
            sVa.inc(V.bn_aggr(mvc[0:r, :], stats6[0:r, :]))
            A.wait_ge(sVa.h, sVa.n)
            sAa.inc(A.activation(out=sd_c[0:r, :], in_=mvc[0:r, 1:2],
                                 func=AF.Sqrt, bias=epsc[0:r, :], scale=1.0))
            V.wait_ge(sAa.h, sAa.n)
            VC(V.reciprocal(rstd_sl[0:r, j:j + 1], sd_c[0:r, :]))
            VC(V.tensor_tensor(out=nmrs_sl[0:r, j:j + 1], in0=mvc[0:r, 0:1],
                               in1=rstd_sl[0:r, j:j + 1], op=ALU.mult))
            sVb.inc(V.tensor_scalar(out=nmrs_sl[0:r, j:j + 1],
                                    in0=nmrs_sl[0:r, j:j + 1], scalar1=-1.0,
                                    scalar2=None, op0=ALU.mult))
            A.wait_ge(sVb.h, sVb.n)
            A.wait_ge(sVc.h, n_amax)
            if j >= 2:
                A.wait_ge(sPt.h, 8 * (j - 1))
            AC(A.activation(out=xnf[0:r, :], in_=xtile[buf][0:r, :],
                            func=AF.Identity, bias=nmrs_sl[0:r, j:j + 1],
                            scale=rstd_sl[0:r, j:j + 1]))
            sAb.inc(A.copy(xnb[buf][0:r, :], xnf[0:r, :]))
            pha_load(j + 2, gate=sAb.n)
            V.wait_ge(sAb.h, sAb.n)
            if j <= NT:
                own = (64, 128) if j == 0 else ((0, 64) if j == NT else (0, 128))
                VC(V.tensor_reduce(out=amx_c[own[0]:own[1], :],
                                   in_=xnf[own[0]:own[1], :], axis=AX.X,
                                   op=ALU.max, apply_absolute_value=True))
                V.wait_ge(sVc.h, sVc.n)
                sVc.inc(V.tensor_tensor(out=gacc[own[0]:own[1], :],
                                        in0=gacc[own[0]:own[1], :],
                                        in1=amx_c[own[0]:own[1], :],
                                        op=ALU.max))
                n_amax += 1
            P.wait_ge(sAb.h, sAb.n)
            if j == 0:
                P.wait_ge(ldi.h, 32)
            for f in range(FCH):
                P.wait_ge(sVd.h, max(0, sVd.n - 1))
                sPt.inc(P.transpose(pb_bf[f % 2][0:128, 0:r],
                                    xnb[buf][0:r, f * 128:(f + 1) * 128],
                                    ident_bf[0:r, 0:r]))
                V.wait_ge(sPt.h, sPt.n)
                if j <= NT:
                    dst = xn_fm[:, f * KT + j * 128: f * KT + j * 128 + r]
                else:
                    dst = xng_fm[:, f * NG:(f + 1) * NG]
                sVd.inc(V.tensor_copy(dst, pb_bf[f % 2][0:128, 0:r]))

        nc.leave_named_scope("PHA", _sc_PHA[0], False)
        _sc_PHB = nc.enter_named_scope("PHB", False)
        P.wait_ge(sVc.h, sVc.n)
        sPt.inc(P.transpose(pb[2].ap()[0:1, 0:128], gacc[:], ident_f[:]))
        V.wait_ge(sPt.h, sPt.n)
        sVc.inc(V.tensor_reduce(out=red1[0:1, 0:1],
                                in_=pb[2].ap()[0:1, 0:128], axis=AX.X,
                                op=ALU.max))
        G.wait_ge(sVc.h, sVc.n)
        cc1s.inc(G.dma_start(out=cc1_in, in_=red1[:]), 16)
        G.wait_ge(cc1s.h, cc1s.n)
        cc1c.inc(G.collective_compute(
            "AllReduce", ALU.max, replica_groups=[list(range(NCORES))],
            ins=[cc1_in], outs=[cc1_out]))
        G.wait_ge(cc1c.h, cc1c.n)
        cc1s.inc(G.dma_start(out=gmax_sb[:], in_=cc1_out), 16)

        nc.leave_named_scope("PHB", _sc_PHB[0], False)
        _sc_PHC = nc.enter_named_scope("PHC", False)
        P.wait_ge(sVd.h, sVd.n)
        P.wait_ge(ldwa.h, ldwa.n)

        last_evac = {0: 0, 1: 0}

        def proj_group(out_dst, lhs, rhs, bank_idx, rows=128):
            bank = pb[bank_idx]
            width = rhs[0].shape[-1]
            P.wait_ge(sVq.h, last_evac[bank_idx])
            mm = None
            for i in range(len(lhs)):
                mm = P.matmul(bank.ap()[0:rows, 0:width], lhs[i], rhs[i],
                              start=(i == 0), stop=(i == len(lhs) - 1))
            sPq.inc(mm)
            V.wait_ge(sPq.h, sPq.n)
            sVq.inc(V.tensor_copy(out_dst, bank.ap()[0:rows, 0:width]))
            last_evac[bank_idx] = sVq.n

        bk = 0
        for m in range(FCH):
            for t in range(T4):
                lhs = [wq_sb[:, f * D + m * 128:(f * D + (m + 1) * 128)]
                       for f in range(FCH)]
                rhs = [xn_fm[:, f * KT + 64 + t * 512:
                             f * KT + 64 + (t + 1) * 512] for f in range(FCH)]
                proj_group(q_fm[:, m * T + t * 512: m * T + (t + 1) * 512],
                           lhs, rhs, bk % 2)
                bk += 1
        def proj_group2(dst_a, dst_b, lhs, rhs, bank_idx):
            bank = pb[bank_idx]
            width = rhs[0].shape[-1]
            P.wait_ge(sVq.h, last_evac[bank_idx])
            mm = None
            for i in range(len(lhs)):
                mm = P.matmul(bank.ap()[:, 0:width], lhs[i], rhs[i],
                              start=(i == 0), stop=(i == len(lhs) - 1))
            sPq.inc(mm)
            V.wait_ge(sPq.h, sPq.n)
            V.tensor_copy(dst_a, bank.ap()[0:64, 0:width])
            sVq.inc(V.tensor_copy(dst_b, bank.ap()[64:128, 0:width]))
            last_evac[bank_idx] = sVq.n

        kslices = [(t * 512, 512) for t in range(KT // 512)] + [(KT - 64, 64)]
        for m in range(2):
            for (o, w) in kslices:
                lhs = [wk_sb[:, f * KVD + m * 128: f * KVD + (m + 1) * 128]
                       for f in range(FCH)]
                rhs = [xn_fm[:, f * KT + o: f * KT + o + w] for f in range(FCH)]
                proj_group2(k_dup[0:64, 2 * m * KT + o: 2 * m * KT + o + w],
                            k_dup[64:128, (2 * m + 1) * KT + o:
                                  (2 * m + 1) * KT + o + w],
                            lhs, rhs, bk % 2)
                bk += 1
        for m in range(2):
            lhs = [wk_sb[:, f * KVD + m * 128: f * KVD + (m + 1) * 128]
                   for f in range(FCH)]
            rhs = [xng_fm[:, f * NG:(f + 1) * NG] for f in range(FCH)]
            proj_group2(kg_dup[0:64, 2 * m * NG:(2 * m + 1) * NG],
                        kg_dup[64:128, (2 * m + 1) * NG:(2 * m + 2) * NG],
                        lhs, rhs, bk % 2)
            bk += 1
        for j in range(NTT):
            r = tile_rows(j)
            lhs = [xn_fm[:, f * KT + j * 128: f * KT + j * 128 + r]
                   for f in range(FCH)]
            rhs = [wv_sb[:, f * KVD:(f + 1) * KVD] for f in range(FCH)]
            proj_group(v_tm[0:r, j * KVD:(j + 1) * KVD], lhs, rhs, bk % 2,
                       rows=r)
            bk += 1
        lhs = [xng_fm[:, f * NG:(f + 1) * NG] for f in range(FCH)]
        rhs = [wv_sb[:, f * KVD:(f + 1) * KVD] for f in range(FCH)]
        proj_group(vg_tm[:], lhs, rhs, bk % 2, rows=NG)

        G.wait_ge(sVq.h, sVq.n)
        for g in range(KVH):
            if g % 2 == 0:
                gdup.inc(G.dma_start(out=k_dup[64:128, g * KT:(g + 1) * KT],
                                     in_=k_dup[0:64, g * KT:(g + 1) * KT]),
                         16)
                gdup.inc(G.dma_start(out=kg_dup[64:128, g * NG:(g + 1) * NG],
                                     in_=kg_dup[0:64, g * NG:(g + 1) * NG]),
                         16)
            else:
                gdup.inc(G.dma_start(out=k_dup[0:64, g * KT:(g + 1) * KT],
                                     in_=k_dup[64:128, g * KT:(g + 1) * KT]),
                         16)
                gdup.inc(G.dma_start(out=kg_dup[0:64, g * NG:(g + 1) * NG],
                                     in_=kg_dup[64:128, g * NG:(g + 1) * NG]),
                         16)

        nc.leave_named_scope("PHC", _sc_PHC[0], False)
        s1.close()

        jobs = []
        s_pieces = ((0, 1024), (1024, 1024), (2048, IDIM - 2048))
        for c in range(FCH):
            for pc, (o, w) in enumerate(s_pieces):
                jobs.append(("sp", wd, c, pc, o, w))
        n_sp = len(jobs)
        for c in range(NCHUNK):
            jobs.append(("q", wg, wgq_s, c, swg_all, D))
            jobs.append(("q", wu, wuq_s, c, swu_all, D))

        st_state = {"load_idx": 0, "quant_idx": 0,
                    "a_done": [], "v_done": []}

        NR = 3
        ldffs = [ldff0, ldff1, ldff2]
        stwl = [stw, stwb]

        def emit_job_load(i):
            if i >= len(jobs):
                return
            job = jobs[i]
            if i >= NR:
                S.wait_ge(sAw.h, st_state["a_done"][i - NR])
                S.wait_ge(sVw.h, st_state["v_done"][i - NR])
            par = ldffs[i % NR]
            if job[0] == "q":
                _, src, _, c, _, width = job
                for q4 in range(0, width, 256):
                    par.inc(S.dma_start(
                        out=wbuf[i % NR][:, q4:q4 + 256],
                        in_=src[c * 128:(c + 1) * 128, q4:q4 + 256]), 16)
            elif job[0] == "sp":
                _, src, c, pc, o, w = job
                for q4 in range(0, w, 256):
                    wq4 = min(256, w - q4)
                    par.inc(S.dma_start(
                        out=wbuf[i % NR][:, q4:q4 + wq4],
                        in_=src[c * 128:(c + 1) * 128,
                                o + q4:o + q4 + wq4]), 16)
            else:
                c = job[1]
                for q4 in range(0, D, 256):
                    par.inc(S.dma_start(
                        out=wbuf[i % NR][:, q4:q4 + 256],
                        in_=wdT[c * 128:(c + 1) * 128, q4:q4 + 256]), 16)

        def emit_job_quant(i):
            job = jobs[i]
            buf = i % NR
            par = ldffs[buf]
            V.wait_ge(par.h, par.n)
            if job[0] == "q":
                _, src, dst, c, sc_dst, width = job
                V.wait_ge(gsc.h, gsc.n)
                VC(V.tensor_reduce(out=wamax[:], in_=wbuf[buf][:, 0:width],
                                   axis=AX.X, op=ALU.max,
                                   apply_absolute_value=True))
                VC(V.reciprocal(wrec[:], wamax[:]))
                VC(V.tensor_scalar(out=wsc127[:], in0=wrec[:], scalar1=QMAX,
                                   scalar2=None, op0=ALU.mult))
                sVw.inc(V.tensor_scalar(out=sc_dst[:, c:c + 1], in0=wamax[:],
                                        scalar1=1.0 / QMAX, scalar2=None,
                                        op0=ALU.mult))
                for j, o in enumerate(range(0, width, 256)):
                    V.wait_ge(sAw.h,
                              max(0, sAw.n - 1))
                    sVw.inc(V.scalar_tensor_tensor(
                        out=wi_s[j % 2][:], in0=wbuf[buf][:, o:o + 256],
                        scalar=wsc127[:, 0:1], in1=wbuf[buf][:, o:o + 256],
                        op0=ALU.mult, op1=ALU.bypass))
                    A.wait_ge(sVw.h, sVw.n)
                    stp = stwl[j % 2]
                    A.wait_ge(stp.h, stp.n)
                    sAw.inc(A.copy(wqb_s[j % 2][:], wi_s[j % 2][:]))
                    S.wait_ge(sAw.h, sAw.n)
                    stp.inc(S.dma_start(
                        out=dst[c * 128:(c + 1) * 128, o:o + 256],
                        in_=wqb_s[j % 2][:]), 16)
            elif job[0] == "sp":
                _, src, c, pc, o, w = job
                V.wait_ge(sVw.h, sVw.n)
                if pc == 0:
                    sVw.inc(V.tensor_reduce(out=wamax[:],
                                            in_=wbuf[buf][:, 0:w],
                                            axis=AX.X, op=ALU.max,
                                            apply_absolute_value=True))
                else:
                    VC(V.tensor_reduce(out=wam2[:], in_=wbuf[buf][:, 0:w],
                                       axis=AX.X, op=ALU.max,
                                       apply_absolute_value=True))
                    sVw.inc(V.tensor_tensor(out=wamax[:], in0=wamax[:],
                                            in1=wam2[:], op=ALU.max))
                if pc == 2:
                    V.wait_ge(gsc.h, gsc.n)
                    V.wait_ge(sVw.h, sVw.n)
                    VC(V.tensor_scalar(out=wsc127[:], in0=wamax[:],
                                       scalar1=1.0 / QMAX, scalar2=None,
                                       op0=ALU.mult))
                    VC(V.reciprocal(wrec[:], wamax[:]))
                    sVw.inc(V.tensor_scalar(out=wrec[:], in0=wrec[:],
                                            scalar1=QMAX, scalar2=None,
                                            op0=ALU.mult))
                    G.wait_ge(sVw.h, sVw.n)
                    gsc.inc(G.dma_start(out=swd_s[0:1, c * 128:(c + 1) * 128],
                                        in_=wsc127[:]), 16)
                    gsc.inc(G.dma_start(out=swdr_s[0:1,
                                               c * 128:(c + 1) * 128],
                                        in_=wrec[:]), 16)
            else:
                c = job[1]
                for j, o in enumerate(range(0, D, 256)):
                    V.wait_ge(sAw.h, max(0, sAw.n - 1))
                    sVw.inc(V.tensor_tensor(out=wi_s[j % 2][:],
                                            in0=wbuf[buf][:, o:o + 256],
                                            in1=swdr_row[:, o:o + 256],
                                            op=ALU.mult))
                    A.wait_ge(sVw.h, sVw.n)
                    stp = stwl[j % 2]
                    A.wait_ge(stp.h, stp.n)
                    sAw.inc(A.copy(wqb_s[j % 2][:], wi_s[j % 2][:]))
                    S.wait_ge(sAw.h, sAw.n)
                    stp.inc(S.dma_start(
                        out=wdqT_s[c * 128:(c + 1) * 128, o:o + 256],
                        in_=wqb_s[j % 2][:]), 16)
            st_state["a_done"].append(sAw.n)
            st_state["v_done"].append(sVw.n)

        def pump_wq():
            qi = st_state["quant_idx"]
            if qi >= len(jobs):
                return
            while st_state["load_idx"] <= min(qi + NR - 1, len(jobs) - 1):
                emit_job_load(st_state["load_idx"])
                st_state["load_idx"] += 1
            emit_job_quant(qi)
            st_state["quant_idx"] = qi + 1

        _sc_PHD = nc.enter_named_scope("PHD", False)
        P.wait_ge(gdup.h, gdup.n)
        P.wait_ge(ldc.h, ldc.n)

        units = [(p, qq) for p in range(FCH) for qq in range(QQ)]
        NU = len(units)
        GOF = 192 + NG
        pump_state = {"bcast": False}

        def pump2():
            if (st_state["quant_idx"] >= n_sp and not pump_state["bcast"]):
                pump_state["bcast"] = True
                S.wait_ge(gsc.h, gsc.n)
                ssc.inc(S.dma_start(out=swdr_row[:],
                                    in_=swdr_s.partition_broadcast(128)), 16)
                V.wait_ge(ssc.h, ssc.n)
                for c in range(NCHUNK):
                    jobs.append(("t", c))
            pump_wq()

        def emit_scores(u):
            p, qq = units[u]
            pi = u % 2
            g = p // 2
            scA = pb[pi]
            scB = pb[2 + pi]
            P.wait_ge(sAe.h, max(0, u - 1))
            lhsA = q_fm[0:64, p * T + qq * 128: p * T + (qq + 1) * 128]
            lhsB = q_fm[64:128, p * T + qq * 128: p * T + (qq + 1) * 128]
            kw = k_dup[:, g * KT + qq * 128: g * KT + qq * 128 + 192]
            msrc = masksq0_sb if qq == 0 else masks_sb
            P.matmul(scA.ap()[:, 0:192], lhsA, kw[0:64, :],
                     start=True, stop=False, skip_group_check=True)
            P.matmul(scA.ap()[:, 0:192], ident_bf[:],
                     msrc[:, (2 * p) * 192:(2 * p + 1) * 192],
                     start=False, stop=True, skip_group_check=True)
            P.matmul(scA.ap()[:, 192:192 + NG], lhsA,
                     kg_dup[0:64, g * NG:(g + 1) * NG],
                     start=False, stop=True, skip_group_check=True)
            P.matmul(scB.ap()[:, 0:192], lhsB, kw[64:128, :],
                     start=True, stop=False, skip_group_check=True)
            P.matmul(scB.ap()[:, 0:192], ident_bf[:],
                     msrc[:, (2 * p + 1) * 192:(2 * p + 2) * 192],
                     start=False, stop=True, skip_group_check=True)
            sPs.inc(P.matmul(scB.ap()[:, 192:192 + NG], lhsB,
                             kg_dup[64:128, g * NG:(g + 1) * NG],
                             start=False, stop=True, skip_group_check=True))

        def emit_exps(u):
            pi = u % 2
            scA = pb[pi]
            scB = pb[2 + pi]
            A.wait_ge(sPs.h, u + 1)
            A.wait_ge(sVp.h, max(0, u - 1))
            A.activation(out=p_all[pi][:, 0:192], in_=scA.ap()[:, 0:192],
                         func=AF.Exp, bias=0.0, scale=c0125[:, 0:1],
                         accum_out=sums_c[pi][:, 0:1])
            A.activation(out=p_all[pi][:, 192:192 + NG],
                         in_=scA.ap()[:, 192:192 + NG],
                         func=AF.Exp, bias=0.0,
                         scale=c0125[:, 0:1], accum_out=sums_c[pi][:, 1:2])
            A.activation(out=p_all[pi][:, GOF:GOF + 192],
                         in_=scB.ap()[:, 0:192],
                         func=AF.Exp, bias=0.0, scale=c0125[:, 0:1],
                         accum_out=sums_c[pi][:, 2:3])
            sAe.inc(A.activation(out=p_all[pi][:, GOF + 192:2 * GOF],
                                 in_=scB.ap()[:, 192:192 + NG],
                                 func=AF.Exp, bias=0.0,
                                 scale=c0125[:, 0:1],
                                 accum_out=sums_c[pi][:, 3:4]))

        def emit_stage_b(u):
            pi = u % 2
            V.wait_ge(sAe.h, u + 1)
            V.wait_ge(sPtr.h, max(0, u - 1))
            VC(V.reciprocal(rec_c[pi][:], sums_c[pi][:]))
            V.scalar_tensor_tensor(out=pn_all[pi][:, 0:192],
                                   in0=p_all[pi][:, 0:192],
                                   scalar=rec_c[pi][:, 0:1],
                                   in1=p_all[pi][:, 0:192],
                                   op0=ALU.mult, op1=ALU.bypass)
            V.scalar_tensor_tensor(out=pn_all[pi][:, 192:192 + NG],
                                   in0=p_all[pi][:, 192:192 + NG],
                                   scalar=rec_c[pi][:, 1:2],
                                   in1=p_all[pi][:, 192:192 + NG],
                                   op0=ALU.mult, op1=ALU.bypass)
            V.scalar_tensor_tensor(out=pn_all[pi][:, GOF:GOF + 192],
                                   in0=p_all[pi][:, GOF:GOF + 192],
                                   scalar=rec_c[pi][:, 2:3],
                                   in1=p_all[pi][:, GOF:GOF + 192],
                                   op0=ALU.mult, op1=ALU.bypass)
            sVp.inc(V.scalar_tensor_tensor(
                out=pn_all[pi][:, GOF + 192:2 * GOF],
                in0=p_all[pi][:, GOF + 192:2 * GOF],
                scalar=rec_c[pi][:, 3:4],
                in1=p_all[pi][:, GOF + 192:2 * GOF],
                op0=ALU.mult, op1=ALU.bypass))

        def emit_tr_av(u):
            p, qq = units[u]
            pi = u % 2
            g = p // 2
            P.wait_ge(sVp.h, u + 1)
            P.wait_ge(sAu.h, max(0, u - 1))
            P.wait_ge(sVav.h, max(0, u - 1))
            P.transpose(pb_bf[4 + pi][0:128, 0:128],
                        pn_all[pi][:, 0:128], ident_bf[:])
            P.matmul(pb_bf[4 + pi][0:128, 128:256],
                     pn_all[pi][:, GOF:GOF + 128], ident_bf[:],
                     is_transpose=True, start=False, stop=True,
                     skip_group_check=True)
            P.transpose(pb_bf[6 + pi][0:64, 0:128],
                        pn_all[pi][:, 128:192], ident_bf[:])
            P.matmul(pb_bf[6 + pi][0:64, 128:256],
                     pn_all[pi][:, GOF + 128:GOF + 192], ident_bf[:],
                     is_transpose=True, start=False, stop=True,
                     skip_group_check=True)
            P.transpose(pb_bf[4 + pi][0:NG, 256:384],
                        pn_all[pi][:, 192:192 + NG], ident_bf[:])
            sPtr.inc(P.matmul(pb_bf[4 + pi][0:NG, 384:512],
                              pn_all[pi][:, GOF + 192:2 * GOF], ident_bf[:],
                              is_transpose=True, start=False, stop=True,
                              skip_group_check=True))
            A.wait_ge(sPtr.h, u + 1)
            A.wait_ge(sPav.h, max(0, u - 1))
            sAt.inc(A.copy(ptgs[pi][0:NG, :], pb_bf[4 + pi][0:NG, 256:512]))
            A.copy(pt1s[pi][:], pb_bf[4 + pi][0:128, 0:256])
            sAu.inc(A.copy(pt2s[pi][:], pb_bf[6 + pi][0:64, 0:256]))
            avb = pb[6 + pi]
            P.wait_ge(sAu.h, u + 1)
            mm = None
            for hh2 in range(2):
                tp = (0, 0) if hh2 == 0 else (0, 64)
                oro = (0, 64) if hh2 == 0 else (64, 128)
                out_av = avb.ap()[oro[0]:oro[1], 0:128]
                P.matmul(out_av,
                         v_tm[:, qq * KVD + g * HD: qq * KVD + (g + 1) * HD],
                         pt1s[pi][:, hh2 * 128:(hh2 + 1) * 128],
                         start=True, stop=False,
                         tile_position=tp, skip_group_check=True)
                P.matmul(out_av,
                         v_tm[0:64, (qq + 1) * KVD + g * HD:
                              (qq + 1) * KVD + (g + 1) * HD],
                         pt2s[pi][0:64, hh2 * 128:(hh2 + 1) * 128],
                         start=False, stop=False,
                         tile_position=tp, skip_group_check=True)
                mm = P.matmul(out_av,
                              vg_tm[:, g * HD:(g + 1) * HD],
                              ptgs[pi][0:NG, hh2 * 128:(hh2 + 1) * 128],
                              start=False, stop=True,
                              tile_position=tp, skip_group_check=True)
            sPav.inc(mm)

        def emit_avcopy(u):
            p, qq = units[u]
            pi = u % 2
            avb = pb[6 + pi]
            V.wait_ge(sPav.h, u + 1)
            sVav.inc(V.tensor_copy(
                attn_fm[:, p * T + qq * 128: p * T + (qq + 1) * 128],
                avb.ap()[:, 0:128]))

        for i in range(NU + 3):
            if i >= 3:
                emit_avcopy(i - 3)
            if 2 <= i < NU + 2:
                emit_stage_b(i - 2)
                emit_tr_av(i - 2)
            if i < NU:
                emit_scores(i)
            if 1 <= i <= NU:
                emit_exps(i - 1)
            pump2()
        while st_state["quant_idx"] < len(jobs) or not pump_state["bcast"]:
            pump2()

        nc.leave_named_scope("PHD", _sc_PHD[0], False)
        _sc_PHE2 = nc.enter_named_scope("PHE2", False)
        nc.leave_named_scope("PHE2", _sc_PHE2[0], False)
        _sc_PHF = nc.enter_named_scope("PHF", False)
        A.wait_ge(cc1s.h, cc1s.n)
        A.wait_ge(ldc.h, ldc.n)
        sAq.inc(A.activation(out=sig_c[:], in_=clipx_sb[:], func=AF.Sigmoid,
                             bias=0.0, scale=1.0))
        V.wait_ge(sAq.h, sAq.n)
        VC(V.tensor_tensor(out=sx_c[:], in0=sig_c[:], in1=gmax_sb[0:1, 0:1],
                           op=ALU.mult))
        VC(V.tensor_scalar(out=sx_c[:], in0=sx_c[:], scalar1=1.0 / QMAX,
                           scalar2=None, op0=ALU.mult))
        sVq2.inc(V.reciprocal(rsx_c[:], sx_c[:]))
        G.wait_ge(sVq2.h, sVq2.n)
        gsc.inc(G.dma_start(out=sc1_s[0:1, 0:1], in_=rsx_c[:]), 16)
        gsc.inc(G.dma_start(out=sc1_s[0:1, 1:2], in_=sx_c[:]), 16)
        S.wait_ge(gsc.h, gsc.n)
        ssc.inc(S.dma_start(out=bc1[:, 0:2],
                            in_=sc1_s[0:1, 0:2].partition_broadcast(128)), 16)
        V.wait_ge(ssc.h, ssc.n)
        V.wait_ge(sVw.h, sVw.n)
        V.scalar_tensor_tensor(out=sug_all[:], in0=swg_all[:],
                               scalar=bc1[:, 1:2], in1=swg_all[:],
                               op0=ALU.mult, op1=ALU.bypass)
        sVq2.inc(V.scalar_tensor_tensor(out=suu_all[:], in0=swu_all[:],
                                        scalar=bc1[:, 1:2], in1=swu_all[:],
                                        op0=ALU.mult, op1=ALU.bypass))

        nc.leave_named_scope("PHF", _sc_PHF[0], False)
        s2.close()
        scope_barrier(stw, stwb, gsc, sVav, sAe, sVp)

        sxint = ExitStack()
        Xint = sxint.enter_context(
            nc.sbuf_tensor("Xint", [128, FCH * XP], BF16, side="right"))

        s3 = ExitStack()
        sb3 = lambda name, shape, dt: s3.enter_context(
            nc.sbuf_tensor(name, shape, dt))
        xt2 = [sb3(f"xt2_{i}", [128, D], F32) for i in range(2)]
        xq_t = sb3("xq_t", [128, D], F32)
        xq_i = sb3("xq_i", [128, D], I32)
        xq_b = sb3("xq_b", [128, D], BF16)
        wo_sb = sb3("wo_sb", [128, FCH * D], BF16)
        xres = [sb3(f"xres{i}", [128, D], F32) for i in range(2)]
        resid = [sb3(f"resid{i}", [128, D], F32) for i in range(2)]

        _sc_PHG1 = nc.enter_named_scope("PHG1", False)
        def g1_load(j, gate=None):
            if j >= NTT:
                return
            r2 = tile_rows(j)
            buf2 = j % 2
            if gate is not None:
                S.wait_ge(sAq.h, gate)
            for q4 in range(0, D, 256):
                [lxa, lxb][buf2].inc(
                    S.dma_start(out=xt2[buf2][0:r2, q4:q4 + 256],
                                in_=x_ext[j * 128:j * 128 + r2,
                                          q4:q4 + 256]), 16)

        g1_load(0)
        g1_load(1)
        for j in range(NTT):
            r = tile_rows(j)
            buf = j % 2
            V.wait_ge([lxa, lxb][buf].h, [lxa, lxb][buf].n)
            V.wait_ge(sAq.h, sAq.n)
            V.scalar_tensor_tensor(out=rstd2[0:r, :],
                                   in0=rstd_sl[0:r, j:j + 1],
                                   scalar=bc1[0:r, 0:1],
                                   in1=rstd_sl[0:r, j:j + 1],
                                   op0=ALU.mult, op1=ALU.bypass)
            sVq2.inc(V.scalar_tensor_tensor(out=nmrs2[0:r, :],
                                            in0=nmrs_sl[0:r, j:j + 1],
                                            scalar=bc1[0:r, 0:1],
                                            in1=nmrs_sl[0:r, j:j + 1],
                                            op0=ALU.mult, op1=ALU.bypass))
            A.wait_ge(sVq2.h, sVq2.n)
            sAq.inc(A.activation(out=xq_t[0:r, :], in_=xt2[buf][0:r, :],
                                 func=AF.Identity, bias=nmrs2[0:r, 0:1],
                                 scale=rstd2[0:r, 0:1]))
            g1_load(j + 2, gate=sAq.n)
            V.wait_ge(sAq.h, sAq.n)
            VC(V.tensor_copy(xq_i[0:r, :], xq_t[0:r, :]))
            V.wait_ge(stq.h, stq.n)
            sVq2.inc(V.tensor_scalar(out=xq_b[0:r, :], in0=xq_i[0:r, :],
                                     scalar1=QMAX, scalar2=-QMAX - 1.0,
                                     op0=ALU.min, op1=ALU.max))
            S.wait_ge(sVq2.h, sVq2.n)
            stq.inc(S.dma_start(out=xq_s[j * 128:j * 128 + r, :],
                                in_=xq_b[0:r, :]), 16)
        V.wait_ge(stq.h, stq.n)
        sVq2.inc(V.memset(xq_b[0:64, :], 0.0))
        S.wait_ge(sVq2.h, sVq2.n)
        stq.inc(S.dma_start(out=xq_s[T + 64:XP, :], in_=xq_b[0:64, :]), 16)
        S.wait_ge(stq.h, stq.n)
        for f in range(FCH):
            trx.inc(S.dma_start(out=Xint[:, f * XP:(f + 1) * XP],
                                in_=xq_s[:, f * 128:(f + 1) * 128],
                                transpose=True), 16)

        nc.leave_named_scope("PHG1", _sc_PHG1[0], False)
        _sc_PHG2 = nc.enter_named_scope("PHG2", False)
        for f in range(FCH):
            ldwa.inc(G.dma_start(out=wo_sb[:, f * D:(f + 1) * D],
                                 in_=Wo[f * 128:(f + 1) * 128, :]), 16)
        P.wait_ge(ldwa.h, ldwa.n)
        P.wait_ge(sVav.h, sVav.n)
        def g2_load(tt, gate=None):
            if tt >= T // 128:
                return
            buf2 = tt % 2
            if gate is not None:
                S.wait_ge(sVo.h, gate)
            for q4 in range(0, D, 256):
                [lxa, lxb][buf2].inc(
                    S.dma_start(out=xres[buf2][:, q4:q4 + 256],
                                in_=x_ext[64 + tt * 128: 64 + (tt + 1) * 128,
                                          q4:q4 + 256]), 16)

        g2_load(0)
        g2_load(1)
        for tt in range(T // 128):
            buf = tt % 2
            for m2 in range(2):
                bank = pb[m2]
                P.wait_ge(sVo.h, max(0, sVo.n - 1))
                mm = None
                for f in range(FCH):
                    mm = P.matmul(
                        bank.ap()[:, 0:512],
                        attn_fm[:, f * T + tt * 128: f * T + (tt + 1) * 128],
                        wo_sb[:, f * D + m2 * 512: f * D + (m2 + 1) * 512],
                        start=(f == 0), stop=(f == FCH - 1))
                sPo.inc(mm)
                V.wait_ge(sPo.h, sPo.n)
                V.wait_ge([lxa, lxb][buf].h, [lxa, lxb][buf].n)
                if tt >= 2 and m2 == 0:
                    V.wait_ge(sty.h, sty.n)
                sVo.inc(V.tensor_tensor(
                    out=resid[buf][:, m2 * 512:(m2 + 1) * 512],
                    in0=bank.ap()[:, 0:512],
                    in1=xres[buf][:, m2 * 512:(m2 + 1) * 512], op=ALU.add))
            g2_load(tt + 2, gate=sVo.n)
            S.wait_ge(sVo.h, sVo.n)
            sty.inc(S.dma_start(out=y[tt * 128:(tt + 1) * 128, :],
                                in_=resid[buf][:]), 16)

        nc.leave_named_scope("PHG2", _sc_PHG2[0], False)
        s3.close()
        sattn.close()
        scope_barrier(sty, stq, trx, sVo, sVq2, sAq)

        s4 = ExitStack()
        sb4 = lambda name, shape, dt: s4.enter_context(
            nc.sbuf_tensor(name, shape, dt))
        assert n_spill <= n_keep
        h_sb = sb4("h_sb", [128, max(1, n_keep) * T], F32)
        hsp_buf = [sb4(f"hsp{i}", [128, T], F32) for i in range(2)]
        hint_view = h_sb.ap().bitcast(BF16)

        s5 = ExitStack()
        sb5 = lambda name, shape, dt: s5.enter_context(
            nc.sbuf_tensor(name, shape, dt))
        wstr = [sb5(f"wstr{i}", [128, FCH * 512], BF16) for i in range(3)]
        silu_sb = sb5("silu_sb", [128, 512], F32)

        _sc_PHH = nc.enter_named_scope("PHH", False)
        groups = []
        c0 = 0
        while c0 < NCHUNK:
            groups.append(list(range(c0, min(c0 + 4, NCHUNK))))
            c0 += 4
        cum_chunks = [0]
        for grp in groups:
            cum_chunks.append(cum_chunks[-1] + len(grp))

        S.wait_ge(stw.h, stw.n)
        S.wait_ge(stwb.h, stwb.n)
        A.wait_ge(sVq2.h, sVq2.n)

        def emit_witem(k):
            gi, kind = k // 2, k % 2
            grp = groups[gi]
            iw = len(grp) * 128
            srcw = wgq_s if kind == 0 else wuq_s
            if k >= 3:
                sem_p = sPf if (k - 3) % 2 == 0 else sPu
                S.wait_ge(sem_p.h, cum_chunks[(k - 3) // 2 + 1] * T4)
            sl = trwl[k % 3]
            for f in range(FCH):
                sl.inc(S.dma_start(
                    out=wstr[k % 3][:, f * 512: f * 512 + iw],
                    in_=srcw[grp[0] * 128: grp[0] * 128 + iw,
                             f * 128:(f + 1) * 128], transpose=True), 16)
            witem_done.append((sl, sl.n))

        witem_done = []
        sp_i = 0
        emit_witem(0)
        emit_witem(1)
        for gi, grp in enumerate(groups):
            if 2 * gi + 2 < 2 * len(groups):
                emit_witem(2 * gi + 2)
            if 2 * gi + 3 < 2 * len(groups):
                emit_witem(2 * gi + 3)
            gslot = (2 * gi) % 3
            uslot = (2 * gi + 1) % 3
            if gi == 0:
                P.wait_ge(trx.h, trx.n)
            for ici, ic in enumerate(grp):
                il = ici * 128
                if ici == 0:
                    P.wait_ge(witem_done[2 * gi][0].h, witem_done[2 * gi][1])
                cno = cum_chunks[gi] + ici
                for t in range(T4):
                    P.wait_ge(sAf.h, max(0, (cno - 1) * T4 + t + 1))
                    mm = None
                    for f in range(FCH):
                        mm = P.matmul(
                            pb[t].ap()[:, 0:512],
                            wstr[gslot][:, f * 512 + il: f * 512 + il + 128],
                            Xint[:, f * XP + 64 + t * 512:
                                 f * XP + 64 + (t + 1) * 512],
                            start=(f == 0), stop=(f == FCH - 1))
                    sPf.inc(mm)
                if ici == 0:
                    P.wait_ge(witem_done[2 * gi + 1][0].h,
                              witem_done[2 * gi + 1][1])
                for t in range(T4):
                    P.wait_ge(sVh.h, max(0, (cno - 1) * T4 + t + 1))
                    mm = None
                    for f in range(FCH):
                        mm = P.matmul(
                            pb[4 + t].ap()[:, 0:512],
                            wstr[uslot][:, f * 512 + il: f * 512 + il + 128],
                            Xint[:, f * XP + 64 + t * 512:
                                 f * XP + 64 + (t + 1) * 512],
                            start=(f == 0), stop=(f == FCH - 1))
                    sPu.inc(mm)
                spill = ic >= n_keep
                for t in range(T4):
                    A.wait_ge(sPf.h, cno * T4 + t + 1)
                    A.wait_ge(sVh.h, max(0, cno * T4 + t))
                    sAf.inc(A.activation(out=silu_sb[:],
                                         in_=pb[t].ap()[:, 0:512],
                                         func=AF.Silu, bias=0.0,
                                         scale=sug_all[:, ic:ic + 1]))
                    V.wait_ge(sAf.h, cno * T4 + t + 1)
                    V.wait_ge(sPu.h, cno * T4 + t + 1)
                    if spill:
                        if sp_i >= 2:
                            V.wait_ge(gsp.h, gsp.n)
                        hdst = hsp_buf[sp_i % 2][:, t * 512:(t + 1) * 512]
                    else:
                        hdst = h_sb[:, ic * T + t * 512: ic * T + (t + 1) * 512]
                    sVh.inc(V.scalar_tensor_tensor(out=hdst,
                                               in0=pb[4 + t].ap()[:, 0:512],
                                               scalar=suu_all[:, ic:ic + 1],
                                               in1=silu_sb[:], op0=ALU.mult,
                                               op1=ALU.mult))
                    V.wait_ge(sVh.h, sVh.n)
                    VC(V.tensor_reduce(out=amx_c[:], in_=hdst, axis=AX.X,
                                       op=ALU.max, apply_absolute_value=True))
                    V.wait_ge(sVf.h, sVf.n)
                    sVf.inc(V.tensor_tensor(out=hacc[:], in0=hacc[:],
                                            in1=amx_c[:], op=ALU.max))
                if spill:
                    G.wait_ge(sVf.h, sVf.n)
                    gsp.inc(G.dma_start(
                        out=h_sp[(ic - n_keep) * 128:
                                 (ic - n_keep + 1) * 128, :],
                        in_=hsp_buf[sp_i % 2][:]), 16)
                    sp_i += 1

        nc.leave_named_scope("PHH", _sc_PHH[0], False)
        s5.close()
        sxint.close()
        scope_barrier(sPf, sAf, sVf, gsp)

        s6 = ExitStack()
        sb6 = lambda name, shape, dt: s6.enter_context(
            nc.sbuf_tensor(name, shape, dt))
        wdT_sb = sb6("wdT_sb", [128, NCHUNK * 512], BF16)
        swd_row = sb6("swd_row", [128, D], F32)
        hq_t = sb6("hq_t", [128, 1024], F32)
        hq_i = sb6("hq_i", [128, 1024], I32)
        dout = [sb6(f"dout{i}", [128, 512], F32) for i in range(2)]

        _sc_PHI = nc.enter_named_scope("PHI", False)
        P.wait_ge(sVf.h, sVf.n)
        sPt.inc(P.transpose(pb[2].ap()[0:1, 0:128], hacc[:], ident_f[:]))
        V.wait_ge(sPt.h, sPt.n)
        sVc.inc(V.tensor_reduce(out=red2[0:1, 0:1],
                                in_=pb[2].ap()[0:1, 0:128], axis=AX.X,
                                op=ALU.max))
        G.wait_ge(gsp.h, gsp.n)
        G.wait_ge(sVc.h, sVc.n)
        cc2s.inc(G.dma_start(out=cc2_in, in_=red2[:]), 16)
        G.wait_ge(cc2s.h, cc2s.n)
        cc2c.inc(G.collective_compute(
            "AllReduce", ALU.max, replica_groups=[list(range(NCORES))],
            ins=[cc2_in], outs=[cc2_out]))
        G.wait_ge(cc2c.h, cc2c.n)
        cc2s.inc(G.dma_start(out=hmax_sb[:], in_=cc2_out), 16)

        def load_wdT_half(m2):
            for ic in range(NCHUNK):
                trd.inc(S.dma_start(
                    out=wdT_sb[:, ic * 512:(ic + 1) * 512],
                    in_=wdqT_s[ic * 128:(ic + 1) * 128,
                               m2 * 512:(m2 + 1) * 512]), 16)

        S.wait_ge(sty.h, sty.n)
        load_wdT_half(0)
        ssc.inc(S.dma_start(out=swd_row[:],
                            in_=swd_s.partition_broadcast(128)), 16)

        A.wait_ge(cc2s.h, cc2s.n)
        sAj.inc(A.activation(out=sig_c[:], in_=clipd_sb[:], func=AF.Sigmoid,
                             bias=0.0, scale=1.0))
        V.wait_ge(sAj.h, sAj.n)
        VC(V.tensor_tensor(out=sh_c[:], in0=sig_c[:], in1=hmax_sb[0:1, 0:1],
                           op=ALU.mult))
        VC(V.tensor_scalar(out=sh_c[:], in0=sh_c[:], scalar1=1.0 / QMAX,
                           scalar2=None, op0=ALU.mult))
        sVj.inc(V.reciprocal(rsh_c[:], sh_c[:]))
        G.wait_ge(sVj.h, sVj.n)
        gsc.inc(G.dma_start(out=sc2_s[0:1, 0:1], in_=rsh_c[:]), 16)
        gsc.inc(G.dma_start(out=sc2_s[0:1, 1:2], in_=sh_c[:]), 16)
        S.wait_ge(gsc.h, gsc.n)
        ssc.inc(S.dma_start(out=bc2[:, 0:2],
                            in_=sc2_s[0:1, 0:2].partition_broadcast(128)), 16)

        nc.leave_named_scope("PHI", _sc_PHI[0], False)
        _sc_PHJ = nc.enter_named_scope("PHJ", False)
        def hint_ap(ic, lo, wlen):
            if ic < n_keep:
                base = 2 * ic * T
            else:
                base = (2 * (ic - n_keep) + 1) * T
            return hint_view[:, base + lo: base + lo + wlen]

        A.wait_ge(ssc.h, ssc.n)
        rl_i = 0
        for ic in range(NCHUNK):
            if ic >= n_keep:
                gbuf = rl_i % 2
                G.wait_ge(sVj.h, sVj.n)
                grl.inc(G.dma_start(
                    out=hsp_buf[gbuf][:],
                    in_=h_sp[(ic - n_keep) * 128:(ic - n_keep + 1) * 128, :]),
                    16)
                src_t, src_off = hsp_buf[gbuf], 0
                rl_i += 1
            else:
                src_t, src_off = h_sb, ic * T
            for lo in range(0, T, 1024):
                w = min(1024, T - lo)
                if ic >= n_keep:
                    A.wait_ge(grl.h, grl.n)
                A.wait_ge(sVx.h, sVx.n)
                sAj.inc(A.activation(
                    out=hq_t[:, 0:w], in_=src_t[:, src_off + lo: src_off + lo + w],
                    func=AF.Identity, bias=0.0, scale=bc2[:, 0:1]))
                V.wait_ge(sAj.h, sAj.n)
                V.wait_ge(sVj.h, sVj.n)
                VC(V.tensor_copy(hq_i[:, 0:w], hq_t[:, 0:w]))
                sVj.inc(V.tensor_scalar(out=hint_ap(ic, lo, w), in0=hq_i[:, 0:w],
                                        scalar1=QMAX, scalar2=-QMAX - 1.0,
                                        op0=ALU.min, op1=ALU.max))

        nc.leave_named_scope("PHJ", _sc_PHJ[0], False)
        _sc_PHK = nc.enter_named_scope("PHK", False)
        P.wait_ge(sVj.h, sVj.n)
        G.wait_ge(sty.h, sty.n)
        idx = 0
        for m2 in range(2):
            if m2 == 1:
                S.wait_ge(sPd.h, sPd.n)
                load_wdT_half(1)
            P.wait_ge(trd.h, trd.n)
            for tt in range(T // 128):
                bank = pb[idx % 2]
                P.wait_ge(sVe.h, max(0, sVe.n - 1))
                mm = None
                for ic in range(NCHUNK):
                    mm = P.matmul(bank.ap()[:, 0:512],
                                  hint_ap(ic, tt * 128, 128),
                                  wdT_sb[:, ic * 512:(ic + 1) * 512],
                                  start=(ic == 0), stop=(ic == NCHUNK - 1))
                sPd.inc(mm)
                V.wait_ge(sPd.h, sPd.n)
                if idx >= 2:
                    V.wait_ge(acc.h, acc.n)
                dbuf = dout[idx % 2]
                sVe.inc(V.scalar_tensor_tensor(
                    out=dbuf[:], in0=bank.ap()[:, 0:512],
                    scalar=bc2[:, 1:2],
                    in1=swd_row[:, m2 * 512:(m2 + 1) * 512],
                    op0=ALU.mult, op1=ALU.mult))
                G.wait_ge(sVe.h, sVe.n)
                acc.inc(G.dma_start(
                    out=y[tt * 128:(tt + 1) * 128, m2 * 512:(m2 + 1) * 512],
                    in_=dbuf[:], accum_op=ALU.add), 16)
                idx += 1

        nc.leave_named_scope("PHK", _sc_PHK[0], False)
        G.wait_ge(acc.h, acc.n)
        G.wait_ge(sty.h, sty.n)

        s6.close()
        s4.close()

    return nc


def _alibi_slopes(n_heads):
    start = 2.0 ** (-2.0 ** (-(math.log2(n_heads) - 3)))
    return np.array([start * start ** i for i in range(n_heads)],
                    dtype=np.float32)


def _host_prep(x, Wq, Wk, Wv, Wo, clip_x, clip_d, wgm, wum, wdm, T):
    B, N, C = x.shape
    slopes = _alibi_slopes(HEADS)
    i_idx = np.arange(128)[:, None]
    j_idx = np.arange(192)[None, :]
    off = (j_idx - 64 - i_idx).astype(np.float32)
    allowed = (j_idx >= i_idx) & (j_idx <= i_idx + 64)
    mask_all = np.where(allowed[None],
                        8.0 * slopes[:, None, None] * off[None], NEG)
    mask_all = np.ascontiguousarray(mask_all.astype(np.float32))
    mask_kill = mask_all.copy()
    mask_kill[:, :, 0:64] = NEG
    wdm = np.asarray(wdm, np.float32)

    common = {
        "Wq": np.ascontiguousarray(Wq, dtype=np.float32),
        "Wk": np.ascontiguousarray(Wk, dtype=np.float32),
        "Wv": np.ascontiguousarray(Wv, dtype=np.float32),
        "Wo": np.ascontiguousarray(Wo, dtype=np.float32),
        "wg": np.ascontiguousarray(wgm, dtype=np.float32),
        "wu": np.ascontiguousarray(wum, dtype=np.float32),
        "wd": np.ascontiguousarray(wdm),
        "wdT": np.ascontiguousarray(wdm.T),
        "clip_x": np.asarray(clip_x, np.float32).reshape(1, 1),
        "clip_d": np.asarray(clip_d, np.float32).reshape(1, 1),
        "masks": mask_all,
    }
    in_maps = []
    for c in range(NCORES):
        b, hh = c // 2, c % 2
        s0 = hh * T
        if hh == 0:
            halo = np.zeros((64, C), np.float32)
        else:
            halo = np.asarray(x[b, s0 - 64:s0, :], np.float32)
        x_core = np.concatenate(
            [halo, np.asarray(x[b, s0:s0 + T, :], np.float32)], axis=0)
        m = dict(common)
        m["x_ext"] = np.ascontiguousarray(x_core)
        m["x_glob"] = np.ascontiguousarray(np.asarray(x[b, ::WIN, :],
                                                      np.float32))
        m["masks_q0"] = mask_kill if hh == 0 else mask_all
        in_maps.append(m)
    return in_maps


def kernel(L0_attn_x, L0_attn_Wq, L0_attn_Wk, L0_attn_Wv, L0_attn_Wo,
           L0_ffn_clip_x, L0_ffn_clip_down, L0_ffn_w_gate, L0_ffn_w_up,
           L0_ffn_w_down):
    T = 2048
    x = np.asarray(L0_attn_x, np.float32)
    B, N, C = x.shape
    in_maps = _host_prep(x, L0_attn_Wq, L0_attn_Wk, L0_attn_Wv, L0_attn_Wo,
                         L0_ffn_clip_x, L0_ffn_clip_down, L0_ffn_w_gate,
                         L0_ffn_w_up, L0_ffn_w_down, T)
    nc = build_program(T=T)
    res = run_bass_kernel_spmd(nc, in_maps, core_ids=list(range(NCORES)))
    out = np.empty((B, N, C), np.float32)
    for c in range(NCORES):
        b, hh = c // 2, c % 2
        out[b, hh * T:(hh + 1) * T, :] = res.results[c]["y"]
    return out


# version 23
# speedup vs baseline: 1.0091x; 1.0059x over previous
import math
import numpy as np
import ml_dtypes
from contextlib import ExitStack

import concourse.bass as bass
import concourse.mybir as mybir
from concourse.bass_utils import run_bass_kernel_spmd

F32 = mybir.dt.float32
BF16 = mybir.dt.bfloat16
I32 = mybir.dt.int32
AF = mybir.ActivationFunctionType
ALU = mybir.AluOpType
AX = mybir.AxisListType

D = 1024
HEADS = 16
KVH = 4
HD = 64
KVD = 256
WIN = 64
IDIM = 2816
NCHUNK = IDIM // 128
FCH = D // 128
NCORES = 8
EPS = 1e-5
SCALE = HD ** -0.5
QMAX = 127.0
NEG = -1.0e30


class Sem:

    def __init__(self, ctx, nc, name):
        self.h = ctx.enter_context(nc.semaphore(name))
        self.n = 0

    def inc(self, instr, k=1):
        instr.then_inc(self.h, k)
        self.n += k
        return instr


def build_program(T=2048, n_spill=6):
    assert T % 512 == 0
    NT = (T + 64) // 128
    NTT = NT + 1
    QQ = T // 128
    T4 = T // 512
    KT = T + 64
    XP = T + 128
    NG = (2 * T) // WIN
    n_keep = NCHUNK - n_spill

    nc = bass.Bass("TRN2", target_bir_lowering=False, debug=False)

    x_ext = nc.dram_tensor("x_ext", [KT, D], F32, kind="ExternalInput").ap()
    x_glob = nc.dram_tensor("x_glob", [NG, D], F32, kind="ExternalInput").ap()
    Wq = nc.dram_tensor("Wq", [D, D], F32, kind="ExternalInput").ap()
    Wk = nc.dram_tensor("Wk", [D, KVD], F32, kind="ExternalInput").ap()
    Wv = nc.dram_tensor("Wv", [D, KVD], F32, kind="ExternalInput").ap()
    Wo = nc.dram_tensor("Wo", [D, D], F32, kind="ExternalInput").ap()
    wg = nc.dram_tensor("wg", [IDIM, D], F32, kind="ExternalInput").ap()
    wu = nc.dram_tensor("wu", [IDIM, D], F32, kind="ExternalInput").ap()
    wd = nc.dram_tensor("wd", [D, IDIM], F32, kind="ExternalInput").ap()
    wdT = nc.dram_tensor("wdT", [IDIM, D], F32, kind="ExternalInput").ap()
    clip_x = nc.dram_tensor("clip_x", [1, 1], F32, kind="ExternalInput").ap()
    clip_d = nc.dram_tensor("clip_d", [1, 1], F32, kind="ExternalInput").ap()
    masks = nc.dram_tensor("masks", [HEADS, 128, 192], F32,
                           kind="ExternalInput").ap()
    qt0k = nc.dram_tensor("qt0k", [128, 64], F32, kind="ExternalInput").ap()
    y = nc.dram_tensor("y", [T, D], F32, kind="ExternalOutput").ap()

    wgq_s = nc.dram_tensor("wgq_s", [IDIM, D], BF16, kind="Internal").ap()
    wuq_s = nc.dram_tensor("wuq_s", [IDIM, D], BF16, kind="Internal").ap()
    wdqT_s = nc.dram_tensor("wdqT_s", [IDIM, D], BF16, kind="Internal").ap()
    xq_s = nc.dram_tensor("xq_s", [XP, D], BF16, kind="Internal").ap()
    h_sp = nc.dram_tensor("h_sp", [max(1, n_spill) * 128, T], F32,
                          kind="Internal").ap()
    swd_s = nc.dram_tensor("swd_s", [1, D], F32, kind="Internal").ap()
    swdr_s = nc.dram_tensor("swdr_s", [1, D], F32, kind="Internal").ap()
    sc1_s = nc.dram_tensor("sc1_s", [1, 8], F32, kind="Internal").ap()
    sc2_s = nc.dram_tensor("sc2_s", [1, 8], F32, kind="Internal").ap()
    cc1_in = nc.dram_tensor("cc1_in", [1, 8], F32, kind="Internal").ap()
    cc1_out = nc.dram_tensor("cc1_out", [1, 8], F32, kind="Internal").ap()
    cc2_in = nc.dram_tensor("cc2_in", [1, 8], F32, kind="Internal").ap()
    cc2_out = nc.dram_tensor("cc2_out", [1, 8], F32, kind="Internal").ap()

    ident_bf_d = nc.inline_tensor(np.eye(128, dtype=ml_dtypes.bfloat16),
                                  name="ident_bf_c").ap()
    ident_f_d = nc.inline_tensor(np.eye(128, dtype=np.float32),
                                 name="ident_f_c").ap()

    S, A, V, P, G = nc.sync, nc.scalar, nc.vector, nc.tensor, nc.gpsimd

    ctx = ExitStack()
    with ctx:
        def sb(name, shape, dt, side=None):
            return ctx.enter_context(nc.sbuf_tensor(name, shape, dt,
                                                    side=side))

        def psb(name):
            return ctx.enter_context(nc.psum_tensor(name, [128, 512], F32))

        ident_bf = sb("ident_bf", [128, 128], BF16)
        ident_f = sb("ident_f", [128, 128], F32)
        c0125 = sb("c0125", [128, 1], F32)
        epsc = sb("epsc", [128, 1], F32)
        clipx_sb = sb("clipx_sb", [1, 1], F32)
        clipd_sb = sb("clipd_sb", [1, 1], F32)
        rstd_sl = sb("rstd_sl", [128, NTT + 1], F32)
        nmrs_sl = sb("nmrs_sl", [128, NTT + 1], F32)
        rstd2 = sb("rstd2", [128, 1], F32)
        nmrs2 = sb("nmrs2", [128, 1], F32)
        stats6 = sb("stats6", [128, 12], F32)
        mvc = sb("mvc", [128, 2], F32)
        sd_c = sb("sd_c", [128, 1], F32)
        gacc = sb("gacc", [128, 1], F32)
        hacc = sb("hacc", [128, 1], F32)
        amx_c = sb("amx_c", [128, 1], F32)
        red1 = sb("red1", [1, 8], F32)
        red2 = sb("red2", [1, 8], F32)
        gmax_sb = sb("gmax_sb", [1, 8], F32)
        hmax_sb = sb("hmax_sb", [1, 8], F32)
        sig_c = sb("sig_c", [1, 1], F32)
        sx_c = sb("sx_c", [1, 1], F32)
        rsx_c = sb("rsx_c", [1, 1], F32)
        sh_c = sb("sh_c", [1, 1], F32)
        rsh_c = sb("rsh_c", [1, 1], F32)
        bc1 = sb("bc1", [128, 8], F32)
        bc2 = sb("bc2", [128, 8], F32)
        swg_all = sb("swg_all", [128, NCHUNK], F32)
        swu_all = sb("swu_all", [128, NCHUNK], F32)
        sug_all = sb("sug_all", [128, NCHUNK], F32)
        suu_all = sb("suu_all", [128, NCHUNK], F32)
        wamax = sb("wamax", [128, 1], F32)
        wam2 = sb("wam2", [128, 1], F32)
        wrec = sb("wrec", [128, 1], F32)
        wsc127 = sb("wsc127", [128, 1], F32)

        pb = [psb(f"pb{i}") for i in range(8)]
        pb_bf = [b.ap().bitcast(BF16) for b in pb]

        names = ("ld0 ld1 ldc ldwa ldff0 ldff1 ldff2 sVa sAa sVb sAb sVc sPt sVd "
                 "sPq sVq gdup cc1s cc2s cc1c cc2c sPs sVs sAe sVp sPtr sAt sPav sVav "
                 "sVw sAw stw sAq sVq2 stq trx sPo sVo sty trw sPf sAf sVf "
                 "gsp trd grl sAj sVj sPd sVe acc gsc ssc lxa lxb ldi sVx sAx trw1 trw2 stwb sVh sPu sAu").split()
        sm = {n: Sem(ctx, nc, n) for n in names}
        (ld0, ld1, ldc, ldwa, ldff0, ldff1, ldff2, sVa, sAa, sVb, sAb, sVc, sPt,
         sVd, sPq, sVq, gdup, cc1s, cc2s, cc1c, cc2c, sPs, sVs, sAe, sVp, sPtr, sAt,
         sPav, sVav, sVw, sAw, stw, sAq, sVq2, stq, trx, sPo, sVo, sty,
         trw, sPf, sAf, sVf, gsp, trd, grl, sAj, sVj, sPd, sVe, acc, gsc,
         ssc, lxa, lxb, ldi, sVx, sAx, trw1, trw2, stwb, sVh, sPu, sAu) = (sm[n] for n in names)
        trwl = [trw, trw1, trw2]

        def VC(instr):
            sVx.inc(instr)
            V.wait_ge(sVx.h, sVx.n)
            return instr

        def AC(instr):
            sAx.inc(instr)
            A.wait_ge(sAx.h, sAx.n)
            return instr

        def scope_barrier(*sems):
            for _eng in (V, A, S, P, G):
                for _s in sems:
                    _eng.wait_ge(_s.h, _s.n)

        ldi.inc(S.dma_start(out=ident_bf[:], in_=ident_bf_d), 16)
        ldi.inc(S.dma_start(out=ident_f[:], in_=ident_f_d), 16)
        ldc.inc(S.dma_start(out=clipx_sb[:], in_=clip_x), 16)
        ldc.inc(S.dma_start(out=clipd_sb[:], in_=clip_d), 16)
        V.memset(c0125[:], SCALE)
        V.memset(epsc[:], EPS)
        V.memset(gacc[:], 0.0)
        V.memset(hacc[:], 0.0)
        V.memset(red1[:], 0.0)
        V.memset(red2[:], 0.0)

        sattn = ExitStack()
        attn_fm = sattn.enter_context(
            nc.sbuf_tensor("attn_fm", [128, FCH * T], BF16))

        s2 = ExitStack()
        sb2 = lambda name, shape, dt: s2.enter_context(
            nc.sbuf_tensor(name, shape, dt))
        sb2r = lambda name, shape, dt: s2.enter_context(
            nc.sbuf_tensor(name, shape, dt, side="right"))
        q_fm = sb2("q_fm", [128, FCH * T], BF16)
        k_dup = sb2("k_dup", [128, KVH * KT], BF16)
        kg_dup = sb2r("kg_dup", [128, KVH * NG], BF16)
        v_tm = sb2("v_tm", [128, NTT * KVD], BF16)
        vg_tm = sb2r("vg_tm", [max(NG, 1), KVD], BF16)
        masks_sb = sb2r("masks_sb", [128, HEADS * 192], BF16)
        qt0k_sb = sb2("qt0k_sb", [128, 64], BF16)
        p_all = [sb2r(f"p_all{i}", [128, 512], BF16) for i in range(4)]
        pn_all = [sb2r(f"pn_all{i}", [128, 512], BF16) for i in range(4)]
        pt1s = [sb2r(f"pt1s{i}", [128, 256], BF16) for i in range(2)]
        pt2s = [sb2r(f"pt2s{i}", [64, 256], BF16) for i in range(2)]
        ptgs = [sb2r(f"ptgs{i}", [64, 256], BF16) for i in range(2)]
        mx_c = [sb2(f"mx_c{i}", [128, 2], F32) for i in range(2)]
        nmx_c = [sb2(f"nmx_c{i}", [128, 2], F32) for i in range(2)]
        sums_c = [sb2(f"sums_c{i}", [128, 4], F32) for i in range(4)]
        rec_c = [sb2(f"rec_c{i}", [128, 4], F32) for i in range(4)]
        wbuf = [sb2r(f"wbuf{i}", [128, 1024], F32) for i in range(3)]
        wi_s = [sb2r(f"wi_s{i}", [128, 256], I32) for i in range(2)]
        wqb_s = [sb2r(f"wqb_s{i}", [128, 256], BF16) for i in range(2)]
        swdr_row = sb2r("swdr_row", [128, D], F32)

        s1 = ExitStack()
        sb1 = lambda name, shape, dt: s1.enter_context(
            nc.sbuf_tensor(name, shape, dt))
        xtile = [sb1(f"xtile{i}", [128, D], F32) for i in range(2)]
        xnf = sb1("xnf", [128, D], F32)
        xnb = [sb1(f"xnb{i}", [128, D], BF16) for i in range(2)]
        xn_fm = sb1("xn_fm", [128, FCH * KT], BF16)
        xng_fm = sb1("xng_fm", [128, FCH * NG], BF16)
        wq_sb = sb1("wq_sb", [128, FCH * D], BF16)
        wk_sb = sb1("wk_sb", [128, FCH * KVD], BF16)
        wv_sb = sb1("wv_sb", [128, FCH * KVD], BF16)

        for f in range(FCH):
            ldwa.inc(G.dma_start(out=wq_sb[:, f * D:(f + 1) * D],
                                 in_=Wq[f * 128:(f + 1) * 128, :]), 16)
            ldwa.inc(G.dma_start(out=wk_sb[:, f * KVD:(f + 1) * KVD],
                                 in_=Wk[f * 128:(f + 1) * 128, :]), 16)
            ldwa.inc(G.dma_start(out=wv_sb[:, f * KVD:(f + 1) * KVD],
                                 in_=Wv[f * 128:(f + 1) * 128, :]), 16)
        for hh in range(HEADS):
            ldc.inc(G.dma_start(
                out=masks_sb[:, hh * 192:(hh + 1) * 192],
                in_=masks[hh]), 16)
        ldc.inc(G.dma_start(out=qt0k_sb[:], in_=qt0k), 16)

        _sc_PHA = nc.enter_named_scope("PHA", False)
        def tile_rows(j):
            return 128 if j < NT else 64

        ld_par = [ld0, ld1]

        def pha_load(j, gate=None):
            if j > NTT:
                return
            r2 = tile_rows(j) if j <= NT else NG
            buf2 = j % 2
            if gate is not None:
                S.wait_ge(sAb.h, gate)
            src2 = x_ext[j * 128: j * 128 + r2, :] if j <= NT else x_glob
            for q4 in range(0, D, 256):
                ld_par[buf2].inc(S.dma_start(
                    out=xtile[buf2][0:r2, q4:q4 + 256],
                    in_=src2[:, q4:q4 + 256]), 16)

        pha_load(0)
        pha_load(1)
        n_amax = 0
        for j in range(NTT + 1):
            r = tile_rows(j) if j <= NT else NG
            buf = j % 2
            V.wait_ge(ld_par[buf].h, ld_par[buf].n)
            V.bn_stats(stats6[0:r, 0:6], xtile[buf][0:r, 0:512])
            VC(V.bn_stats(stats6[0:r, 6:12], xtile[buf][0:r, 512:1024]))
            sVa.inc(V.bn_aggr(mvc[0:r, :], stats6[0:r, :]))
            A.wait_ge(sVa.h, sVa.n)
            sAa.inc(A.activation(out=sd_c[0:r, :], in_=mvc[0:r, 1:2],
                                 func=AF.Sqrt, bias=epsc[0:r, :], scale=1.0))
            V.wait_ge(sAa.h, sAa.n)
            VC(V.reciprocal(rstd_sl[0:r, j:j + 1], sd_c[0:r, :]))
            VC(V.tensor_tensor(out=nmrs_sl[0:r, j:j + 1], in0=mvc[0:r, 0:1],
                               in1=rstd_sl[0:r, j:j + 1], op=ALU.mult))
            sVb.inc(V.tensor_scalar(out=nmrs_sl[0:r, j:j + 1],
                                    in0=nmrs_sl[0:r, j:j + 1], scalar1=-1.0,
                                    scalar2=None, op0=ALU.mult))
            A.wait_ge(sVb.h, sVb.n)
            A.wait_ge(sVc.h, n_amax)
            if j >= 2:
                A.wait_ge(sPt.h, 8 * (j - 1))
            AC(A.activation(out=xnf[0:r, :], in_=xtile[buf][0:r, :],
                            func=AF.Identity, bias=nmrs_sl[0:r, j:j + 1],
                            scale=rstd_sl[0:r, j:j + 1]))
            sAb.inc(A.copy(xnb[buf][0:r, :], xnf[0:r, :]))
            pha_load(j + 2, gate=sAb.n)
            V.wait_ge(sAb.h, sAb.n)
            if j <= NT:
                own = (64, 128) if j == 0 else ((0, 64) if j == NT else (0, 128))
                VC(V.tensor_reduce(out=amx_c[own[0]:own[1], :],
                                   in_=xnf[own[0]:own[1], :], axis=AX.X,
                                   op=ALU.max, apply_absolute_value=True))
                V.wait_ge(sVc.h, sVc.n)
                sVc.inc(V.tensor_tensor(out=gacc[own[0]:own[1], :],
                                        in0=gacc[own[0]:own[1], :],
                                        in1=amx_c[own[0]:own[1], :],
                                        op=ALU.max))
                n_amax += 1
            P.wait_ge(sAb.h, sAb.n)
            if j == 0:
                P.wait_ge(ldi.h, 32)
            for f in range(FCH):
                P.wait_ge(sVd.h, max(0, sVd.n - 1))
                sPt.inc(P.transpose(pb_bf[f % 2][0:128, 0:r],
                                    xnb[buf][0:r, f * 128:(f + 1) * 128],
                                    ident_bf[0:r, 0:r]))
                V.wait_ge(sPt.h, sPt.n)
                if j <= NT:
                    dst = xn_fm[:, f * KT + j * 128: f * KT + j * 128 + r]
                else:
                    dst = xng_fm[:, f * NG:(f + 1) * NG]
                sVd.inc(V.tensor_copy(dst, pb_bf[f % 2][0:128, 0:r]))

        nc.leave_named_scope("PHA", _sc_PHA[0], False)
        _sc_PHB = nc.enter_named_scope("PHB", False)
        P.wait_ge(sVc.h, sVc.n)
        sPt.inc(P.transpose(pb[2].ap()[0:1, 0:128], gacc[:], ident_f[:]))
        V.wait_ge(sPt.h, sPt.n)
        sVc.inc(V.tensor_reduce(out=red1[0:1, 0:1],
                                in_=pb[2].ap()[0:1, 0:128], axis=AX.X,
                                op=ALU.max))
        G.wait_ge(sVc.h, sVc.n)
        cc1s.inc(G.dma_start(out=cc1_in, in_=red1[:]), 16)
        G.wait_ge(cc1s.h, cc1s.n)
        cc1c.inc(G.collective_compute(
            "AllReduce", ALU.max, replica_groups=[list(range(NCORES))],
            ins=[cc1_in], outs=[cc1_out]))
        G.wait_ge(cc1c.h, cc1c.n)
        cc1s.inc(G.dma_start(out=gmax_sb[:], in_=cc1_out), 16)

        nc.leave_named_scope("PHB", _sc_PHB[0], False)
        _sc_PHC = nc.enter_named_scope("PHC", False)
        P.wait_ge(sVd.h, sVd.n)
        P.wait_ge(ldwa.h, ldwa.n)

        last_evac = {0: 0, 1: 0}

        def proj_group(out_dst, lhs, rhs, bank_idx, rows=128):
            bank = pb[bank_idx]
            width = rhs[0].shape[-1]
            P.wait_ge(sVq.h, last_evac[bank_idx])
            mm = None
            for i in range(len(lhs)):
                mm = P.matmul(bank.ap()[0:rows, 0:width], lhs[i], rhs[i],
                              start=(i == 0), stop=(i == len(lhs) - 1))
            sPq.inc(mm)
            V.wait_ge(sPq.h, sPq.n)
            sVq.inc(V.tensor_copy(out_dst, bank.ap()[0:rows, 0:width]))
            last_evac[bank_idx] = sVq.n

        bk = 0
        for m in range(FCH):
            for t in range(T4):
                lhs = [wq_sb[:, f * D + m * 128:(f * D + (m + 1) * 128)]
                       for f in range(FCH)]
                rhs = [xn_fm[:, f * KT + 64 + t * 512:
                             f * KT + 64 + (t + 1) * 512] for f in range(FCH)]
                proj_group(q_fm[:, m * T + t * 512: m * T + (t + 1) * 512],
                           lhs, rhs, bk % 2)
                bk += 1
        def proj_group2(dst_a, dst_b, lhs, rhs, bank_idx):
            bank = pb[bank_idx]
            width = rhs[0].shape[-1]
            P.wait_ge(sVq.h, last_evac[bank_idx])
            mm = None
            for i in range(len(lhs)):
                mm = P.matmul(bank.ap()[:, 0:width], lhs[i], rhs[i],
                              start=(i == 0), stop=(i == len(lhs) - 1))
            sPq.inc(mm)
            V.wait_ge(sPq.h, sPq.n)
            V.tensor_copy(dst_a, bank.ap()[0:64, 0:width])
            sVq.inc(V.tensor_copy(dst_b, bank.ap()[64:128, 0:width]))
            last_evac[bank_idx] = sVq.n

        kslices = [(t * 512, 512) for t in range(KT // 512)] + [(KT - 64, 64)]
        for m in range(2):
            for (o, w) in kslices:
                lhs = [wk_sb[:, f * KVD + m * 128: f * KVD + (m + 1) * 128]
                       for f in range(FCH)]
                rhs = [xn_fm[:, f * KT + o: f * KT + o + w] for f in range(FCH)]
                proj_group2(k_dup[0:64, 2 * m * KT + o: 2 * m * KT + o + w],
                            k_dup[64:128, (2 * m + 1) * KT + o:
                                  (2 * m + 1) * KT + o + w],
                            lhs, rhs, bk % 2)
                bk += 1
        for m in range(2):
            lhs = [wk_sb[:, f * KVD + m * 128: f * KVD + (m + 1) * 128]
                   for f in range(FCH)]
            rhs = [xng_fm[:, f * NG:(f + 1) * NG] for f in range(FCH)]
            proj_group2(kg_dup[0:64, 2 * m * NG:(2 * m + 1) * NG],
                        kg_dup[64:128, (2 * m + 1) * NG:(2 * m + 2) * NG],
                        lhs, rhs, bk % 2)
            bk += 1
        for j in range(NTT):
            r = tile_rows(j)
            lhs = [xn_fm[:, f * KT + j * 128: f * KT + j * 128 + r]
                   for f in range(FCH)]
            rhs = [wv_sb[:, f * KVD:(f + 1) * KVD] for f in range(FCH)]
            proj_group(v_tm[0:r, j * KVD:(j + 1) * KVD], lhs, rhs, bk % 2,
                       rows=r)
            bk += 1
        lhs = [xng_fm[:, f * NG:(f + 1) * NG] for f in range(FCH)]
        rhs = [wv_sb[:, f * KVD:(f + 1) * KVD] for f in range(FCH)]
        proj_group(vg_tm[:], lhs, rhs, bk % 2, rows=NG)

        G.wait_ge(sVq.h, sVq.n)
        for g in range(KVH):
            if g % 2 == 0:
                gdup.inc(G.dma_start(out=k_dup[64:128, g * KT:(g + 1) * KT],
                                     in_=k_dup[0:64, g * KT:(g + 1) * KT]),
                         16)
                gdup.inc(G.dma_start(out=kg_dup[64:128, g * NG:(g + 1) * NG],
                                     in_=kg_dup[0:64, g * NG:(g + 1) * NG]),
                         16)
            else:
                gdup.inc(G.dma_start(out=k_dup[0:64, g * KT:(g + 1) * KT],
                                     in_=k_dup[64:128, g * KT:(g + 1) * KT]),
                         16)
                gdup.inc(G.dma_start(out=kg_dup[0:64, g * NG:(g + 1) * NG],
                                     in_=kg_dup[64:128, g * NG:(g + 1) * NG]),
                         16)

        nc.leave_named_scope("PHC", _sc_PHC[0], False)
        s1.close()

        jobs = []
        s_pieces = ((0, 1024), (1024, 1024), (2048, IDIM - 2048))
        for c in range(FCH):
            for pc, (o, w) in enumerate(s_pieces):
                jobs.append(("sp", wd, c, pc, o, w))
        n_sp = len(jobs)
        for c in range(NCHUNK):
            jobs.append(("q", wg, wgq_s, c, swg_all, D))
            jobs.append(("q", wu, wuq_s, c, swu_all, D))

        st_state = {"load_idx": 0, "quant_idx": 0,
                    "a_done": [], "v_done": []}

        NR = 3
        ldffs = [ldff0, ldff1, ldff2]
        stwl = [stw, stwb]

        def emit_job_load(i):
            if i >= len(jobs):
                return
            job = jobs[i]
            if i >= NR:
                S.wait_ge(sAw.h, st_state["a_done"][i - NR])
                S.wait_ge(sVw.h, st_state["v_done"][i - NR])
            par = ldffs[i % NR]
            if job[0] == "q":
                _, src, _, c, _, width = job
                for q4 in range(0, width, 256):
                    par.inc(S.dma_start(
                        out=wbuf[i % NR][:, q4:q4 + 256],
                        in_=src[c * 128:(c + 1) * 128, q4:q4 + 256]), 16)
            elif job[0] == "sp":
                _, src, c, pc, o, w = job
                for q4 in range(0, w, 256):
                    wq4 = min(256, w - q4)
                    par.inc(S.dma_start(
                        out=wbuf[i % NR][:, q4:q4 + wq4],
                        in_=src[c * 128:(c + 1) * 128,
                                o + q4:o + q4 + wq4]), 16)
            else:
                c = job[1]
                for q4 in range(0, D, 256):
                    par.inc(S.dma_start(
                        out=wbuf[i % NR][:, q4:q4 + 256],
                        in_=wdT[c * 128:(c + 1) * 128, q4:q4 + 256]), 16)

        def emit_job_quant(i):
            job = jobs[i]
            buf = i % NR
            par = ldffs[buf]
            V.wait_ge(par.h, par.n)
            if job[0] == "q":
                _, src, dst, c, sc_dst, width = job
                V.wait_ge(gsc.h, gsc.n)
                VC(V.tensor_reduce(out=wamax[:], in_=wbuf[buf][:, 0:width],
                                   axis=AX.X, op=ALU.max,
                                   apply_absolute_value=True))
                VC(V.reciprocal(wrec[:], wamax[:]))
                VC(V.tensor_scalar(out=wsc127[:], in0=wrec[:], scalar1=QMAX,
                                   scalar2=None, op0=ALU.mult))
                sVw.inc(V.tensor_scalar(out=sc_dst[:, c:c + 1], in0=wamax[:],
                                        scalar1=1.0 / QMAX, scalar2=None,
                                        op0=ALU.mult))
                for j, o in enumerate(range(0, width, 256)):
                    V.wait_ge(sAw.h,
                              max(0, sAw.n - 1))
                    sVw.inc(V.scalar_tensor_tensor(
                        out=wi_s[j % 2][:], in0=wbuf[buf][:, o:o + 256],
                        scalar=wsc127[:, 0:1], in1=wbuf[buf][:, o:o + 256],
                        op0=ALU.mult, op1=ALU.bypass))
                    A.wait_ge(sVw.h, sVw.n)
                    stp = stwl[j % 2]
                    A.wait_ge(stp.h, stp.n)
                    sAw.inc(A.copy(wqb_s[j % 2][:], wi_s[j % 2][:]))
                    S.wait_ge(sAw.h, sAw.n)
                    stp.inc(S.dma_start(
                        out=dst[c * 128:(c + 1) * 128, o:o + 256],
                        in_=wqb_s[j % 2][:]), 16)
            elif job[0] == "sp":
                _, src, c, pc, o, w = job
                V.wait_ge(sVw.h, sVw.n)
                if pc == 0:
                    sVw.inc(V.tensor_reduce(out=wamax[:],
                                            in_=wbuf[buf][:, 0:w],
                                            axis=AX.X, op=ALU.max,
                                            apply_absolute_value=True))
                else:
                    VC(V.tensor_reduce(out=wam2[:], in_=wbuf[buf][:, 0:w],
                                       axis=AX.X, op=ALU.max,
                                       apply_absolute_value=True))
                    sVw.inc(V.tensor_tensor(out=wamax[:], in0=wamax[:],
                                            in1=wam2[:], op=ALU.max))
                if pc == 2:
                    V.wait_ge(gsc.h, gsc.n)
                    V.wait_ge(sVw.h, sVw.n)
                    VC(V.tensor_scalar(out=wsc127[:], in0=wamax[:],
                                       scalar1=1.0 / QMAX, scalar2=None,
                                       op0=ALU.mult))
                    VC(V.reciprocal(wrec[:], wamax[:]))
                    sVw.inc(V.tensor_scalar(out=wrec[:], in0=wrec[:],
                                            scalar1=QMAX, scalar2=None,
                                            op0=ALU.mult))
                    G.wait_ge(sVw.h, sVw.n)
                    gsc.inc(G.dma_start(out=swd_s[0:1, c * 128:(c + 1) * 128],
                                        in_=wsc127[:]), 16)
                    gsc.inc(G.dma_start(out=swdr_s[0:1,
                                               c * 128:(c + 1) * 128],
                                        in_=wrec[:]), 16)
            else:
                c = job[1]
                for j, o in enumerate(range(0, D, 256)):
                    V.wait_ge(sAw.h, max(0, sAw.n - 1))
                    sVw.inc(V.tensor_tensor(out=wi_s[j % 2][:],
                                            in0=wbuf[buf][:, o:o + 256],
                                            in1=swdr_row[:, o:o + 256],
                                            op=ALU.mult))
                    A.wait_ge(sVw.h, sVw.n)
                    stp = stwl[j % 2]
                    A.wait_ge(stp.h, stp.n)
                    sAw.inc(A.copy(wqb_s[j % 2][:], wi_s[j % 2][:]))
                    S.wait_ge(sAw.h, sAw.n)
                    stp.inc(S.dma_start(
                        out=wdqT_s[c * 128:(c + 1) * 128, o:o + 256],
                        in_=wqb_s[j % 2][:]), 16)
            st_state["a_done"].append(sAw.n)
            st_state["v_done"].append(sVw.n)

        def pump_wq():
            qi = st_state["quant_idx"]
            if qi >= len(jobs):
                return
            while st_state["load_idx"] <= min(qi + NR - 1, len(jobs) - 1):
                emit_job_load(st_state["load_idx"])
                st_state["load_idx"] += 1
            emit_job_quant(qi)
            st_state["quant_idx"] = qi + 1

        _sc_PHD = nc.enter_named_scope("PHD", False)
        P.wait_ge(gdup.h, gdup.n)
        P.wait_ge(ldc.h, ldc.n)

        units = [(p, qq) for p in range(FCH) for qq in range(QQ)]
        NU = len(units)
        GOF = 192 + NG
        pump_state = {"bcast": False}

        def pump2():
            if (st_state["quant_idx"] >= n_sp and not pump_state["bcast"]):
                pump_state["bcast"] = True
                S.wait_ge(gsc.h, gsc.n)
                ssc.inc(S.dma_start(out=swdr_row[:],
                                    in_=swdr_s.partition_broadcast(128)), 16)
                V.wait_ge(ssc.h, ssc.n)
                for c in range(NCHUNK):
                    jobs.append(("t", c))
            pump_wq()

        def emit_scores(u):
            p, qq = units[u]
            pi = u % 2
            g = p // 2
            scA = pb[pi]
            scB = pb[2 + pi]
            P.wait_ge(sAe.h, max(0, u - 1))
            lhsA = q_fm[0:64, p * T + qq * 128: p * T + (qq + 1) * 128]
            lhsB = q_fm[64:128, p * T + qq * 128: p * T + (qq + 1) * 128]
            kw = k_dup[:, g * KT + qq * 128: g * KT + qq * 128 + 192]
            P.matmul(scA.ap()[:, 0:192], lhsA, kw[0:64, :],
                     start=True, stop=False, skip_group_check=True)
            if qq == 0:
                P.matmul(scA.ap()[:, 0:64], ident_bf[:], qt0k_sb[:],
                         start=False, stop=False, skip_group_check=True)
            P.matmul(scA.ap()[:, 0:192], ident_bf[:],
                     masks_sb[:, (2 * p) * 192:(2 * p + 1) * 192],
                     start=False, stop=True, skip_group_check=True)
            P.matmul(scA.ap()[:, 192:192 + NG], lhsA,
                     kg_dup[0:64, g * NG:(g + 1) * NG],
                     start=False, stop=True, skip_group_check=True)
            P.matmul(scB.ap()[:, 0:192], lhsB, kw[64:128, :],
                     start=True, stop=False, skip_group_check=True)
            if qq == 0:
                P.matmul(scB.ap()[:, 0:64], ident_bf[:], qt0k_sb[:],
                         start=False, stop=False, skip_group_check=True)
            P.matmul(scB.ap()[:, 0:192], ident_bf[:],
                     masks_sb[:, (2 * p + 1) * 192:(2 * p + 2) * 192],
                     start=False, stop=True, skip_group_check=True)
            sPs.inc(P.matmul(scB.ap()[:, 192:192 + NG], lhsB,
                             kg_dup[64:128, g * NG:(g + 1) * NG],
                             start=False, stop=True, skip_group_check=True))

        def emit_exps(u):
            pi = u % 2
            pj = u % 4
            scA = pb[pi]
            scB = pb[2 + pi]
            A.wait_ge(sPs.h, u + 1)
            A.wait_ge(sVp.h, max(0, u - 3))
            A.activation(out=p_all[pj][:, 0:192], in_=scA.ap()[:, 0:192],
                         func=AF.Exp, bias=0.0, scale=c0125[:, 0:1],
                         accum_out=sums_c[pj][:, 0:1])
            A.activation(out=p_all[pj][:, 192:192 + NG],
                         in_=scA.ap()[:, 192:192 + NG],
                         func=AF.Exp, bias=0.0,
                         scale=c0125[:, 0:1], accum_out=sums_c[pj][:, 1:2])
            A.activation(out=p_all[pj][:, GOF:GOF + 192],
                         in_=scB.ap()[:, 0:192],
                         func=AF.Exp, bias=0.0, scale=c0125[:, 0:1],
                         accum_out=sums_c[pj][:, 2:3])
            sAe.inc(A.activation(out=p_all[pj][:, GOF + 192:2 * GOF],
                                 in_=scB.ap()[:, 192:192 + NG],
                                 func=AF.Exp, bias=0.0,
                                 scale=c0125[:, 0:1],
                                 accum_out=sums_c[pj][:, 3:4]))

        def emit_stage_b(u):
            pj = u % 4
            V.wait_ge(sAe.h, u + 1)
            V.wait_ge(sPtr.h, max(0, u - 3))
            VC(V.reciprocal(rec_c[pj][:], sums_c[pj][:]))
            V.scalar_tensor_tensor(out=pn_all[pj][:, 0:192],
                                   in0=p_all[pj][:, 0:192],
                                   scalar=rec_c[pj][:, 0:1],
                                   in1=p_all[pj][:, 0:192],
                                   op0=ALU.mult, op1=ALU.bypass)
            V.scalar_tensor_tensor(out=pn_all[pj][:, 192:192 + NG],
                                   in0=p_all[pj][:, 192:192 + NG],
                                   scalar=rec_c[pj][:, 1:2],
                                   in1=p_all[pj][:, 192:192 + NG],
                                   op0=ALU.mult, op1=ALU.bypass)
            V.scalar_tensor_tensor(out=pn_all[pj][:, GOF:GOF + 192],
                                   in0=p_all[pj][:, GOF:GOF + 192],
                                   scalar=rec_c[pj][:, 2:3],
                                   in1=p_all[pj][:, GOF:GOF + 192],
                                   op0=ALU.mult, op1=ALU.bypass)
            sVp.inc(V.scalar_tensor_tensor(
                out=pn_all[pj][:, GOF + 192:2 * GOF],
                in0=p_all[pj][:, GOF + 192:2 * GOF],
                scalar=rec_c[pj][:, 3:4],
                in1=p_all[pj][:, GOF + 192:2 * GOF],
                op0=ALU.mult, op1=ALU.bypass))

        def emit_tr_av(u):
            p, qq = units[u]
            pi = u % 2
            pj = u % 4
            g = p // 2
            P.wait_ge(sVp.h, u + 1)
            P.wait_ge(sAu.h, max(0, u - 1))
            P.wait_ge(sVav.h, max(0, u - 1))
            P.transpose(pb_bf[4 + pi][0:128, 0:128],
                        pn_all[pj][:, 0:128], ident_bf[:])
            P.matmul(pb_bf[4 + pi][0:128, 128:256],
                     pn_all[pj][:, GOF:GOF + 128], ident_bf[:],
                     is_transpose=True, start=False, stop=True,
                     skip_group_check=True)
            P.transpose(pb_bf[6 + pi][0:64, 0:128],
                        pn_all[pj][:, 128:192], ident_bf[:])
            P.matmul(pb_bf[6 + pi][0:64, 128:256],
                     pn_all[pj][:, GOF + 128:GOF + 192], ident_bf[:],
                     is_transpose=True, start=False, stop=True,
                     skip_group_check=True)
            P.transpose(pb_bf[4 + pi][0:NG, 256:384],
                        pn_all[pj][:, 192:192 + NG], ident_bf[:])
            sPtr.inc(P.matmul(pb_bf[4 + pi][0:NG, 384:512],
                              pn_all[pj][:, GOF + 192:2 * GOF], ident_bf[:],
                              is_transpose=True, start=False, stop=True,
                              skip_group_check=True))
            A.wait_ge(sPtr.h, u + 1)
            A.wait_ge(sPav.h, max(0, u - 1))
            sAt.inc(A.copy(ptgs[pi][0:NG, :], pb_bf[4 + pi][0:NG, 256:512]))
            A.copy(pt1s[pi][:], pb_bf[4 + pi][0:128, 0:256])
            sAu.inc(A.copy(pt2s[pi][:], pb_bf[6 + pi][0:64, 0:256]))
            avb = pb[6 + pi]
            P.wait_ge(sAu.h, u + 1)
            mm = None
            for hh2 in range(2):
                tp = (0, 0) if hh2 == 0 else (0, 64)
                oro = (0, 64) if hh2 == 0 else (64, 128)
                out_av = avb.ap()[oro[0]:oro[1], 0:128]
                P.matmul(out_av,
                         v_tm[:, qq * KVD + g * HD: qq * KVD + (g + 1) * HD],
                         pt1s[pi][:, hh2 * 128:(hh2 + 1) * 128],
                         start=True, stop=False,
                         tile_position=tp, skip_group_check=True)
                P.matmul(out_av,
                         v_tm[0:64, (qq + 1) * KVD + g * HD:
                              (qq + 1) * KVD + (g + 1) * HD],
                         pt2s[pi][0:64, hh2 * 128:(hh2 + 1) * 128],
                         start=False, stop=False,
                         tile_position=tp, skip_group_check=True)
                mm = P.matmul(out_av,
                              vg_tm[:, g * HD:(g + 1) * HD],
                              ptgs[pi][0:NG, hh2 * 128:(hh2 + 1) * 128],
                              start=False, stop=True,
                              tile_position=tp, skip_group_check=True)
            sPav.inc(mm)

        def emit_avcopy(u):
            p, qq = units[u]
            pi = u % 2
            avb = pb[6 + pi]
            V.wait_ge(sPav.h, u + 1)
            sVav.inc(V.tensor_copy(
                attn_fm[:, p * T + qq * 128: p * T + (qq + 1) * 128],
                avb.ap()[:, 0:128]))

        for i in range(NU + 3):
            if i >= 3:
                emit_avcopy(i - 3)
            if 2 <= i < NU + 2:
                emit_stage_b(i - 2)
                emit_tr_av(i - 2)
            if i < NU:
                emit_scores(i)
            if 1 <= i <= NU:
                emit_exps(i - 1)
            pump2()
        while st_state["quant_idx"] < len(jobs) or not pump_state["bcast"]:
            pump2()

        nc.leave_named_scope("PHD", _sc_PHD[0], False)
        _sc_PHE2 = nc.enter_named_scope("PHE2", False)
        nc.leave_named_scope("PHE2", _sc_PHE2[0], False)
        _sc_PHF = nc.enter_named_scope("PHF", False)
        A.wait_ge(cc1s.h, cc1s.n)
        A.wait_ge(ldc.h, ldc.n)
        sAq.inc(A.activation(out=sig_c[:], in_=clipx_sb[:], func=AF.Sigmoid,
                             bias=0.0, scale=1.0))
        V.wait_ge(sAq.h, sAq.n)
        VC(V.tensor_tensor(out=sx_c[:], in0=sig_c[:], in1=gmax_sb[0:1, 0:1],
                           op=ALU.mult))
        VC(V.tensor_scalar(out=sx_c[:], in0=sx_c[:], scalar1=1.0 / QMAX,
                           scalar2=None, op0=ALU.mult))
        sVq2.inc(V.reciprocal(rsx_c[:], sx_c[:]))
        G.wait_ge(sVq2.h, sVq2.n)
        gsc.inc(G.dma_start(out=sc1_s[0:1, 0:1], in_=rsx_c[:]), 16)
        gsc.inc(G.dma_start(out=sc1_s[0:1, 1:2], in_=sx_c[:]), 16)
        S.wait_ge(gsc.h, gsc.n)
        ssc.inc(S.dma_start(out=bc1[:, 0:2],
                            in_=sc1_s[0:1, 0:2].partition_broadcast(128)), 16)
        V.wait_ge(ssc.h, ssc.n)
        V.wait_ge(sVw.h, sVw.n)
        V.scalar_tensor_tensor(out=sug_all[:], in0=swg_all[:],
                               scalar=bc1[:, 1:2], in1=swg_all[:],
                               op0=ALU.mult, op1=ALU.bypass)
        sVq2.inc(V.scalar_tensor_tensor(out=suu_all[:], in0=swu_all[:],
                                        scalar=bc1[:, 1:2], in1=swu_all[:],
                                        op0=ALU.mult, op1=ALU.bypass))

        nc.leave_named_scope("PHF", _sc_PHF[0], False)
        s2.close()
        scope_barrier(stw, stwb, gsc, sVav, sAe, sVp)

        sxint = ExitStack()
        Xint = sxint.enter_context(
            nc.sbuf_tensor("Xint", [128, FCH * XP], BF16, side="right"))

        s3 = ExitStack()
        sb3 = lambda name, shape, dt: s3.enter_context(
            nc.sbuf_tensor(name, shape, dt))
        xt2 = [sb3(f"xt2_{i}", [128, D], F32) for i in range(2)]
        xq_t = sb3("xq_t", [128, D], F32)
        xq_i = sb3("xq_i", [128, D], I32)
        xq_b = sb3("xq_b", [128, D], BF16)
        wo_sb = sb3("wo_sb", [128, FCH * D], BF16)
        xres = [sb3(f"xres{i}", [128, D], F32) for i in range(2)]
        resid = [sb3(f"resid{i}", [128, D], F32) for i in range(2)]

        _sc_PHG1 = nc.enter_named_scope("PHG1", False)
        def g1_load(j, gate=None):
            if j >= NTT:
                return
            r2 = tile_rows(j)
            buf2 = j % 2
            if gate is not None:
                S.wait_ge(sAq.h, gate)
            for q4 in range(0, D, 256):
                [lxa, lxb][buf2].inc(
                    S.dma_start(out=xt2[buf2][0:r2, q4:q4 + 256],
                                in_=x_ext[j * 128:j * 128 + r2,
                                          q4:q4 + 256]), 16)

        g1_load(0)
        g1_load(1)
        for j in range(NTT):
            r = tile_rows(j)
            buf = j % 2
            V.wait_ge([lxa, lxb][buf].h, [lxa, lxb][buf].n)
            V.wait_ge(sAq.h, sAq.n)
            V.scalar_tensor_tensor(out=rstd2[0:r, :],
                                   in0=rstd_sl[0:r, j:j + 1],
                                   scalar=bc1[0:r, 0:1],
                                   in1=rstd_sl[0:r, j:j + 1],
                                   op0=ALU.mult, op1=ALU.bypass)
            sVq2.inc(V.scalar_tensor_tensor(out=nmrs2[0:r, :],
                                            in0=nmrs_sl[0:r, j:j + 1],
                                            scalar=bc1[0:r, 0:1],
                                            in1=nmrs_sl[0:r, j:j + 1],
                                            op0=ALU.mult, op1=ALU.bypass))
            A.wait_ge(sVq2.h, sVq2.n)
            sAq.inc(A.activation(out=xq_t[0:r, :], in_=xt2[buf][0:r, :],
                                 func=AF.Identity, bias=nmrs2[0:r, 0:1],
                                 scale=rstd2[0:r, 0:1]))
            g1_load(j + 2, gate=sAq.n)
            V.wait_ge(sAq.h, sAq.n)
            VC(V.tensor_copy(xq_i[0:r, :], xq_t[0:r, :]))
            V.wait_ge(stq.h, stq.n)
            sVq2.inc(V.tensor_scalar(out=xq_b[0:r, :], in0=xq_i[0:r, :],
                                     scalar1=QMAX, scalar2=-QMAX - 1.0,
                                     op0=ALU.min, op1=ALU.max))
            S.wait_ge(sVq2.h, sVq2.n)
            stq.inc(S.dma_start(out=xq_s[j * 128:j * 128 + r, :],
                                in_=xq_b[0:r, :]), 16)
        V.wait_ge(stq.h, stq.n)
        sVq2.inc(V.memset(xq_b[0:64, :], 0.0))
        S.wait_ge(sVq2.h, sVq2.n)
        stq.inc(S.dma_start(out=xq_s[T + 64:XP, :], in_=xq_b[0:64, :]), 16)
        S.wait_ge(stq.h, stq.n)
        for f in range(FCH):
            trx.inc(S.dma_start(out=Xint[:, f * XP:(f + 1) * XP],
                                in_=xq_s[:, f * 128:(f + 1) * 128],
                                transpose=True), 16)

        nc.leave_named_scope("PHG1", _sc_PHG1[0], False)
        _sc_PHG2 = nc.enter_named_scope("PHG2", False)
        for f in range(FCH):
            ldwa.inc(G.dma_start(out=wo_sb[:, f * D:(f + 1) * D],
                                 in_=Wo[f * 128:(f + 1) * 128, :]), 16)
        P.wait_ge(ldwa.h, ldwa.n)
        P.wait_ge(sVav.h, sVav.n)
        def g2_load(tt, gate=None):
            if tt >= T // 128:
                return
            buf2 = tt % 2
            if gate is not None:
                S.wait_ge(sVo.h, gate)
            for q4 in range(0, D, 256):
                [lxa, lxb][buf2].inc(
                    S.dma_start(out=xres[buf2][:, q4:q4 + 256],
                                in_=x_ext[64 + tt * 128: 64 + (tt + 1) * 128,
                                          q4:q4 + 256]), 16)

        g2_load(0)
        g2_load(1)
        for tt in range(T // 128):
            buf = tt % 2
            for m2 in range(2):
                bank = pb[m2]
                P.wait_ge(sVo.h, max(0, sVo.n - 1))
                mm = None
                for f in range(FCH):
                    mm = P.matmul(
                        bank.ap()[:, 0:512],
                        attn_fm[:, f * T + tt * 128: f * T + (tt + 1) * 128],
                        wo_sb[:, f * D + m2 * 512: f * D + (m2 + 1) * 512],
                        start=(f == 0), stop=(f == FCH - 1))
                sPo.inc(mm)
                V.wait_ge(sPo.h, sPo.n)
                V.wait_ge([lxa, lxb][buf].h, [lxa, lxb][buf].n)
                if tt >= 2 and m2 == 0:
                    V.wait_ge(sty.h, sty.n)
                sVo.inc(V.tensor_tensor(
                    out=resid[buf][:, m2 * 512:(m2 + 1) * 512],
                    in0=bank.ap()[:, 0:512],
                    in1=xres[buf][:, m2 * 512:(m2 + 1) * 512], op=ALU.add))
            g2_load(tt + 2, gate=sVo.n)
            S.wait_ge(sVo.h, sVo.n)
            sty.inc(S.dma_start(out=y[tt * 128:(tt + 1) * 128, :],
                                in_=resid[buf][:]), 16)

        nc.leave_named_scope("PHG2", _sc_PHG2[0], False)
        s3.close()
        sattn.close()
        scope_barrier(sty, stq, trx, sVo, sVq2, sAq)

        s4 = ExitStack()
        sb4 = lambda name, shape, dt: s4.enter_context(
            nc.sbuf_tensor(name, shape, dt))
        assert n_spill <= n_keep
        h_sb = sb4("h_sb", [128, max(1, n_keep) * T], F32)
        hsp_buf = [sb4(f"hsp{i}", [128, T], F32) for i in range(2)]
        hint_view = h_sb.ap().bitcast(BF16)

        s5 = ExitStack()
        sb5 = lambda name, shape, dt: s5.enter_context(
            nc.sbuf_tensor(name, shape, dt))
        wstr = [sb5(f"wstr{i}", [128, FCH * 512], BF16) for i in range(3)]
        silu_sb = sb5("silu_sb", [128, 512], F32)

        _sc_PHH = nc.enter_named_scope("PHH", False)
        groups = []
        c0 = 0
        while c0 < NCHUNK:
            groups.append(list(range(c0, min(c0 + 4, NCHUNK))))
            c0 += 4
        cum_chunks = [0]
        for grp in groups:
            cum_chunks.append(cum_chunks[-1] + len(grp))

        S.wait_ge(stw.h, stw.n)
        S.wait_ge(stwb.h, stwb.n)
        A.wait_ge(sVq2.h, sVq2.n)

        def emit_witem(k):
            gi, kind = k // 2, k % 2
            grp = groups[gi]
            iw = len(grp) * 128
            srcw = wgq_s if kind == 0 else wuq_s
            if k >= 3:
                sem_p = sPf if (k - 3) % 2 == 0 else sPu
                S.wait_ge(sem_p.h, cum_chunks[(k - 3) // 2 + 1] * T4)
            sl = trwl[k % 3]
            for f in range(FCH):
                sl.inc(S.dma_start(
                    out=wstr[k % 3][:, f * 512: f * 512 + iw],
                    in_=srcw[grp[0] * 128: grp[0] * 128 + iw,
                             f * 128:(f + 1) * 128], transpose=True), 16)
            witem_done.append((sl, sl.n))

        witem_done = []
        sp_i = 0
        emit_witem(0)
        emit_witem(1)
        for gi, grp in enumerate(groups):
            if 2 * gi + 2 < 2 * len(groups):
                emit_witem(2 * gi + 2)
            if 2 * gi + 3 < 2 * len(groups):
                emit_witem(2 * gi + 3)
            gslot = (2 * gi) % 3
            uslot = (2 * gi + 1) % 3
            if gi == 0:
                P.wait_ge(trx.h, trx.n)
            for ici, ic in enumerate(grp):
                il = ici * 128
                if ici == 0:
                    P.wait_ge(witem_done[2 * gi][0].h, witem_done[2 * gi][1])
                cno = cum_chunks[gi] + ici
                for t in range(T4):
                    P.wait_ge(sAf.h, max(0, (cno - 1) * T4 + t + 1))
                    mm = None
                    for f in range(FCH):
                        mm = P.matmul(
                            pb[t].ap()[:, 0:512],
                            wstr[gslot][:, f * 512 + il: f * 512 + il + 128],
                            Xint[:, f * XP + 64 + t * 512:
                                 f * XP + 64 + (t + 1) * 512],
                            start=(f == 0), stop=(f == FCH - 1))
                    sPf.inc(mm)
                if ici == 0:
                    P.wait_ge(witem_done[2 * gi + 1][0].h,
                              witem_done[2 * gi + 1][1])
                for t in range(T4):
                    P.wait_ge(sVh.h, max(0, (cno - 1) * T4 + t + 1))
                    mm = None
                    for f in range(FCH):
                        mm = P.matmul(
                            pb[4 + t].ap()[:, 0:512],
                            wstr[uslot][:, f * 512 + il: f * 512 + il + 128],
                            Xint[:, f * XP + 64 + t * 512:
                                 f * XP + 64 + (t + 1) * 512],
                            start=(f == 0), stop=(f == FCH - 1))
                    sPu.inc(mm)
                spill = ic >= n_keep
                for t in range(T4):
                    A.wait_ge(sPf.h, cno * T4 + t + 1)
                    A.wait_ge(sVh.h, max(0, cno * T4 + t))
                    sAf.inc(A.activation(out=silu_sb[:],
                                         in_=pb[t].ap()[:, 0:512],
                                         func=AF.Silu, bias=0.0,
                                         scale=sug_all[:, ic:ic + 1]))
                    V.wait_ge(sAf.h, cno * T4 + t + 1)
                    V.wait_ge(sPu.h, cno * T4 + t + 1)
                    if spill:
                        if sp_i >= 2:
                            V.wait_ge(gsp.h, gsp.n)
                        hdst = hsp_buf[sp_i % 2][:, t * 512:(t + 1) * 512]
                    else:
                        hdst = h_sb[:, ic * T + t * 512: ic * T + (t + 1) * 512]
                    sVh.inc(V.scalar_tensor_tensor(out=hdst,
                                               in0=pb[4 + t].ap()[:, 0:512],
                                               scalar=suu_all[:, ic:ic + 1],
                                               in1=silu_sb[:], op0=ALU.mult,
                                               op1=ALU.mult))
                    V.wait_ge(sVh.h, sVh.n)
                    VC(V.tensor_reduce(out=amx_c[:], in_=hdst, axis=AX.X,
                                       op=ALU.max, apply_absolute_value=True))
                    V.wait_ge(sVf.h, sVf.n)
                    sVf.inc(V.tensor_tensor(out=hacc[:], in0=hacc[:],
                                            in1=amx_c[:], op=ALU.max))
                if spill:
                    G.wait_ge(sVf.h, sVf.n)
                    gsp.inc(G.dma_start(
                        out=h_sp[(ic - n_keep) * 128:
                                 (ic - n_keep + 1) * 128, :],
                        in_=hsp_buf[sp_i % 2][:]), 16)
                    sp_i += 1

        nc.leave_named_scope("PHH", _sc_PHH[0], False)
        s5.close()
        sxint.close()
        scope_barrier(sPf, sAf, sVf, gsp)

        s6 = ExitStack()
        sb6 = lambda name, shape, dt: s6.enter_context(
            nc.sbuf_tensor(name, shape, dt))
        wdT_sb = sb6("wdT_sb", [128, NCHUNK * 512], BF16)
        swd_row = sb6("swd_row", [128, D], F32)
        hq_t = sb6("hq_t", [128, 1024], F32)
        hq_i = sb6("hq_i", [128, 1024], I32)
        dout = [sb6(f"dout{i}", [128, 512], F32) for i in range(2)]

        _sc_PHI = nc.enter_named_scope("PHI", False)
        P.wait_ge(sVf.h, sVf.n)
        sPt.inc(P.transpose(pb[2].ap()[0:1, 0:128], hacc[:], ident_f[:]))
        V.wait_ge(sPt.h, sPt.n)
        sVc.inc(V.tensor_reduce(out=red2[0:1, 0:1],
                                in_=pb[2].ap()[0:1, 0:128], axis=AX.X,
                                op=ALU.max))
        G.wait_ge(gsp.h, gsp.n)
        G.wait_ge(sVc.h, sVc.n)
        cc2s.inc(G.dma_start(out=cc2_in, in_=red2[:]), 16)
        G.wait_ge(cc2s.h, cc2s.n)
        cc2c.inc(G.collective_compute(
            "AllReduce", ALU.max, replica_groups=[list(range(NCORES))],
            ins=[cc2_in], outs=[cc2_out]))
        G.wait_ge(cc2c.h, cc2c.n)
        cc2s.inc(G.dma_start(out=hmax_sb[:], in_=cc2_out), 16)

        def load_wdT_half(m2):
            for ic in range(NCHUNK):
                trd.inc(S.dma_start(
                    out=wdT_sb[:, ic * 512:(ic + 1) * 512],
                    in_=wdqT_s[ic * 128:(ic + 1) * 128,
                               m2 * 512:(m2 + 1) * 512]), 16)

        S.wait_ge(sty.h, sty.n)
        load_wdT_half(0)
        ssc.inc(S.dma_start(out=swd_row[:],
                            in_=swd_s.partition_broadcast(128)), 16)

        A.wait_ge(cc2s.h, cc2s.n)
        sAj.inc(A.activation(out=sig_c[:], in_=clipd_sb[:], func=AF.Sigmoid,
                             bias=0.0, scale=1.0))
        V.wait_ge(sAj.h, sAj.n)
        VC(V.tensor_tensor(out=sh_c[:], in0=sig_c[:], in1=hmax_sb[0:1, 0:1],
                           op=ALU.mult))
        VC(V.tensor_scalar(out=sh_c[:], in0=sh_c[:], scalar1=1.0 / QMAX,
                           scalar2=None, op0=ALU.mult))
        sVj.inc(V.reciprocal(rsh_c[:], sh_c[:]))
        G.wait_ge(sVj.h, sVj.n)
        gsc.inc(G.dma_start(out=sc2_s[0:1, 0:1], in_=rsh_c[:]), 16)
        gsc.inc(G.dma_start(out=sc2_s[0:1, 1:2], in_=sh_c[:]), 16)
        S.wait_ge(gsc.h, gsc.n)
        ssc.inc(S.dma_start(out=bc2[:, 0:2],
                            in_=sc2_s[0:1, 0:2].partition_broadcast(128)), 16)

        nc.leave_named_scope("PHI", _sc_PHI[0], False)
        _sc_PHJ = nc.enter_named_scope("PHJ", False)
        def hint_ap(ic, lo, wlen):
            if ic < n_keep:
                base = 2 * ic * T
            else:
                base = (2 * (ic - n_keep) + 1) * T
            return hint_view[:, base + lo: base + lo + wlen]

        A.wait_ge(ssc.h, ssc.n)
        rl_i = 0
        for ic in range(NCHUNK):
            if ic >= n_keep:
                gbuf = rl_i % 2
                G.wait_ge(sVj.h, sVj.n)
                grl.inc(G.dma_start(
                    out=hsp_buf[gbuf][:],
                    in_=h_sp[(ic - n_keep) * 128:(ic - n_keep + 1) * 128, :]),
                    16)
                src_t, src_off = hsp_buf[gbuf], 0
                rl_i += 1
            else:
                src_t, src_off = h_sb, ic * T
            for lo in range(0, T, 1024):
                w = min(1024, T - lo)
                if ic >= n_keep:
                    A.wait_ge(grl.h, grl.n)
                A.wait_ge(sVx.h, sVx.n)
                sAj.inc(A.activation(
                    out=hq_t[:, 0:w], in_=src_t[:, src_off + lo: src_off + lo + w],
                    func=AF.Identity, bias=0.0, scale=bc2[:, 0:1]))
                V.wait_ge(sAj.h, sAj.n)
                V.wait_ge(sVj.h, sVj.n)
                VC(V.tensor_copy(hq_i[:, 0:w], hq_t[:, 0:w]))
                sVj.inc(V.tensor_scalar(out=hint_ap(ic, lo, w), in0=hq_i[:, 0:w],
                                        scalar1=QMAX, scalar2=-QMAX - 1.0,
                                        op0=ALU.min, op1=ALU.max))

        nc.leave_named_scope("PHJ", _sc_PHJ[0], False)
        _sc_PHK = nc.enter_named_scope("PHK", False)
        P.wait_ge(sVj.h, sVj.n)
        G.wait_ge(sty.h, sty.n)
        idx = 0
        for m2 in range(2):
            if m2 == 1:
                S.wait_ge(sPd.h, sPd.n)
                load_wdT_half(1)
            P.wait_ge(trd.h, trd.n)
            for tt in range(T // 128):
                bank = pb[idx % 2]
                P.wait_ge(sVe.h, max(0, sVe.n - 1))
                mm = None
                for ic in range(NCHUNK):
                    mm = P.matmul(bank.ap()[:, 0:512],
                                  hint_ap(ic, tt * 128, 128),
                                  wdT_sb[:, ic * 512:(ic + 1) * 512],
                                  start=(ic == 0), stop=(ic == NCHUNK - 1))
                sPd.inc(mm)
                V.wait_ge(sPd.h, sPd.n)
                if idx >= 2:
                    V.wait_ge(acc.h, acc.n)
                dbuf = dout[idx % 2]
                sVe.inc(V.scalar_tensor_tensor(
                    out=dbuf[:], in0=bank.ap()[:, 0:512],
                    scalar=bc2[:, 1:2],
                    in1=swd_row[:, m2 * 512:(m2 + 1) * 512],
                    op0=ALU.mult, op1=ALU.mult))
                G.wait_ge(sVe.h, sVe.n)
                acc.inc(G.dma_start(
                    out=y[tt * 128:(tt + 1) * 128, m2 * 512:(m2 + 1) * 512],
                    in_=dbuf[:], accum_op=ALU.add), 16)
                idx += 1

        nc.leave_named_scope("PHK", _sc_PHK[0], False)
        G.wait_ge(acc.h, acc.n)
        G.wait_ge(sty.h, sty.n)

        s6.close()
        s4.close()

    return nc


def _alibi_slopes(n_heads):
    start = 2.0 ** (-2.0 ** (-(math.log2(n_heads) - 3)))
    return np.array([start * start ** i for i in range(n_heads)],
                    dtype=np.float32)


def _host_prep(x, Wq, Wk, Wv, Wo, clip_x, clip_d, wgm, wum, wdm, T):
    B, N, C = x.shape
    slopes = _alibi_slopes(HEADS)
    i_idx = np.arange(128)[:, None]
    j_idx = np.arange(192)[None, :]
    off = (j_idx - 64 - i_idx).astype(np.float32)
    allowed = (j_idx >= i_idx) & (j_idx <= i_idx + 64)
    mask_all = np.where(allowed[None],
                        8.0 * slopes[:, None, None] * off[None], NEG)
    mask_all = np.ascontiguousarray(mask_all.astype(np.float32))
    kill = np.full((128, 64), NEG, np.float32)
    nokill = np.zeros((128, 64), np.float32)
    wdm = np.asarray(wdm, np.float32)

    common = {
        "Wq": np.ascontiguousarray(Wq, dtype=np.float32),
        "Wk": np.ascontiguousarray(Wk, dtype=np.float32),
        "Wv": np.ascontiguousarray(Wv, dtype=np.float32),
        "Wo": np.ascontiguousarray(Wo, dtype=np.float32),
        "wg": np.ascontiguousarray(wgm, dtype=np.float32),
        "wu": np.ascontiguousarray(wum, dtype=np.float32),
        "wd": np.ascontiguousarray(wdm),
        "wdT": np.ascontiguousarray(wdm.T),
        "clip_x": np.asarray(clip_x, np.float32).reshape(1, 1),
        "clip_d": np.asarray(clip_d, np.float32).reshape(1, 1),
        "masks": mask_all,
    }
    in_maps = []
    for c in range(NCORES):
        b, hh = c // 2, c % 2
        s0 = hh * T
        if hh == 0:
            halo = np.zeros((64, C), np.float32)
        else:
            halo = np.asarray(x[b, s0 - 64:s0, :], np.float32)
        x_core = np.concatenate(
            [halo, np.asarray(x[b, s0:s0 + T, :], np.float32)], axis=0)
        m = dict(common)
        m["x_ext"] = np.ascontiguousarray(x_core)
        m["x_glob"] = np.ascontiguousarray(np.asarray(x[b, ::WIN, :],
                                                      np.float32))
        m["qt0k"] = kill if hh == 0 else nokill
        in_maps.append(m)
    return in_maps


def kernel(L0_attn_x, L0_attn_Wq, L0_attn_Wk, L0_attn_Wv, L0_attn_Wo,
           L0_ffn_clip_x, L0_ffn_clip_down, L0_ffn_w_gate, L0_ffn_w_up,
           L0_ffn_w_down):
    T = 2048
    x = np.asarray(L0_attn_x, np.float32)
    B, N, C = x.shape
    in_maps = _host_prep(x, L0_attn_Wq, L0_attn_Wk, L0_attn_Wv, L0_attn_Wo,
                         L0_ffn_clip_x, L0_ffn_clip_down, L0_ffn_w_gate,
                         L0_ffn_w_up, L0_ffn_w_down, T)
    nc = build_program(T=T)
    res = run_bass_kernel_spmd(nc, in_maps, core_ids=list(range(NCORES)))
    out = np.empty((B, N, C), np.float32)
    for c in range(NCORES):
        b, hh = c // 2, c % 2
        out[b, hh * T:(hh + 1) * T, :] = res.results[c]["y"]
    return out
